# version 21
# baseline (speedup 1.0000x reference)
import math
from contextlib import ExitStack

import numpy as np
import ml_dtypes

import concourse.bass as bass
import concourse.mybir as mybir
from concourse.bass_utils import run_bass_kernel_spmd

F32 = mybir.dt.float32
BF16 = mybir.dt.bfloat16
AF = mybir.ActivationFunctionType
ALU = mybir.AluOpType
AX = mybir.AxisListType.X

T = 2048
D = 2048
NT = T // 128
NF = D // 128
NH = 16
NKV = 4
HD = 128
NG = 4
E = 8
IFF = 1408
NIC = IFF // 128
CAP = 5120
EPS = 1e-5
ISQD = 1.0 / math.sqrt(HD)
ROPE_THETA = 10000.0
TH = T // 2
NTH = NT // 2


class Q:

    def __init__(self, nc, eng, sem):
        self.eng = eng
        self.sem = sem
        self.n = 0
        self.waited = {}

    def dma(self, out, in_, **kw):
        i = self.eng.dma_start(out=out, in_=in_, **kw)
        self.n += 16
        i.then_inc(self.sem, 16)
        return ("q", self.sem, self.n)

    def wait(self, tok):
        _wait(self.eng, self.waited, tok)


def _wait(eng, waited, tok):
    if tok is None:
        return
    kind, sem, n = tok
    if n <= 0:
        return
    key = id(sem)
    if waited.get(key, 0) >= n:
        return
    eng.wait_ge(sem, n)
    waited[key] = n


class Eng:

    def __init__(self, obj, sem):
        self.obj = obj
        self.sem = sem
        self.n = 0
        self.last = None
        self.last_tok = None
        self.waited = {}

    def rec(self, instr):
        self.last = instr
        self.last_tok = None
        return instr

    def tok(self):
        if self.last is None:
            return ("e", self.sem, 0)
        if self.last_tok is None:
            self.n += 1
            self.last.then_inc(self.sem, 1)
            self.last_tok = self.n
        return ("e", self.sem, self.last_tok)

    def wait(self, tok):
        _wait(self.obj, self.waited, tok)


class Prog:
    def __init__(self, nc, st):
        self.nc = nc
        self.st = st
        sems = {k: st.enter_context(nc.semaphore("s_" + k))
                for k in ("pe", "dve", "act", "pool", "qs", "qg", "qv")}
        self.pe = Eng(nc.tensor, sems["pe"])
        self.dve = Eng(nc.vector, sems["dve"])
        self.act = Eng(nc.scalar, sems["act"])
        self.pool = Eng(nc.gpsimd, sems["pool"])
        self.qs = Q(nc, nc.sync, sems["qs"])
        self.qg = Q(nc, nc.gpsimd, sems["qg"])
        self.qv = Q(nc, nc.scalar, sems["qv"])
        self.engs = [self.pe, self.dve, self.act, self.pool]

    def mm(self, eng_waits, out, lhsT, rhs, start, stop):
        for w in eng_waits:
            self.pe.wait(w)
        return self.pe.rec(self.nc.tensor.matmul(out, lhsT, rhs, start=start, stop=stop))

    def transp(self, eng_waits, out, in_, ident):
        for w in eng_waits:
            self.pe.wait(w)
        return self.pe.rec(self.nc.tensor.transpose(out, in_, ident))

    def dop(self, name, eng_waits, *args, **kw):
        for w in eng_waits:
            self.dve.wait(w)
        return self.dve.rec(getattr(self.nc.vector, name)(*args, **kw))

    def pop(self, name, eng_waits, *args, **kw):
        for w in eng_waits:
            self.pool.wait(w)
        return self.pool.rec(getattr(self.nc.gpsimd, name)(*args, **kw))

    def aop(self, eng_waits, out, in_, func, **kw):
        for w in eng_waits:
            self.act.wait(w)
        return self.act.rec(self.nc.scalar.activation(out, in_, func, **kw))

    def barrier(self, skip_qg=False):
        toks = [e.tok() for e in self.engs]
        toks.append(("q", self.qs.sem, self.qs.n))
        toks.append(("q", self.qv.sem, self.qv.n))
        if not skip_qg:
            toks.append(("q", self.qg.sem, self.qg.n))
        for e in self.engs:
            for t in toks:
                e.wait(t)
        for t in toks:
            self.qs.wait(t)
            self.qv.wait(t)
            self.qg.wait(t)


def sb(st, nc, name, shape, dtype, side="left"):
    return st.enter_context(nc.sbuf_tensor(name, shape, dtype, side=side))


def sbr(st, nc, name, shape, dtype):
    return sb(st, nc, name, shape, dtype, side="right")


PHASE_MARKS = []


def build_program(dbg=None, upto="all"):
    dbg = dbg or set()
    nc = bass.Bass("TRN2", target_bir_lowering=False)
    PHASE_MARKS.clear()

    def mark(nm):
        PHASE_MARKS.append((nm, sum(1 for _ in nc.all_instructions())))

    x_in = nc.dram_tensor("x", [T, D], F32, kind="ExternalInput").ap()
    wq = nc.dram_tensor("wq", [D, D], F32, kind="ExternalInput").ap()
    wk = nc.dram_tensor("wk", [D, 512], F32, kind="ExternalInput").ap()
    wv = nc.dram_tensor("wv", [D, 512], F32, kind="ExternalInput").ap()
    wo = nc.dram_tensor("wo", [D, D], F32, kind="ExternalInput").ap()
    wrt = nc.dram_tensor("wrt", [D, E], F32, kind="ExternalInput").ap()
    wg_s = nc.dram_tensor("wg_s", [E, D, IFF], F32, kind="ExternalInput").ap()
    wu_s = nc.dram_tensor("wu_s", [E, D, IFF], F32, kind="ExternalInput").ap()
    wd_s = nc.dram_tensor("wd_s", [E, IFF, D], F32, kind="ExternalInput").ap()
    id_bf = nc.dram_tensor("id_bf", [128, 128], BF16, kind="ExternalInput").ap()
    id_f = nc.dram_tensor("id_f", [128, 128], F32, kind="ExternalInput").ap()
    tri16 = nc.dram_tensor("tri16", [128, 128], F32, kind="ExternalInput").ap()
    mkeep = nc.dram_tensor("mkeep", [128, 128], F32, kind="ExternalInput").ap()
    mlow = nc.dram_tensor("mlow", [128, 128], BF16, kind="ExternalInput").ap()
    u1c = nc.dram_tensor("u1c", [128, 128], BF16, kind="ExternalInput").ap()
    u2c = nc.dram_tensor("u2c", [128, 128], BF16, kind="ExternalInput").ap()
    cos_t = nc.dram_tensor("cos_t", [128, NT * 64], F32, kind="ExternalInput").ap()
    sin_t = nc.dram_tensor("sin_t", [128, NT * 64], F32, kind="ExternalInput").ap()
    iota8 = nc.dram_tensor("iota8", [128, E], F32, kind="ExternalInput").ap()
    cvals = nc.dram_tensor("cvals", [128, 4], F32, kind="ExternalInput").ap()

    out = nc.dram_tensor("out", [T, D], F32, kind="ExternalOutput").ap()
    xmid_hbm = nc.dram_tensor("xmid_scr", [TH, D], F32, kind="Internal").ap()
    h2_hbm = nc.dram_tensor("h2_scr", [128, NF * TH], BF16, kind="Internal").ap()
    o_scr = nc.dram_tensor("o_scr", [NG, NT, 128, 512], BF16, kind="Internal").ap()
    wqb = nc.dram_tensor("wqb", [D, D], BF16, kind="Internal").ap()
    wkb = nc.dram_tensor("wkb", [D, 512], BF16, kind="Internal").ap()
    wvb = nc.dram_tensor("wvb", [D, 512], BF16, kind="Internal").ap()
    wob = nc.dram_tensor("wob", [D, D], BF16, kind="Internal").ap()
    wrtb = nc.dram_tensor("wrtb", [D, E], BF16, kind="Internal").ap()
    wgb = nc.dram_tensor("wgb", [E, D, IFF], BF16, kind="Internal").ap()
    wub = nc.dram_tensor("wub", [E, D, IFF], BF16, kind="Internal").ap()
    wdbs = nc.dram_tensor("wdbs", [E, IFF, D], BF16, kind="Internal").ap()

    dbg_out = {}
    if "h_fm" in dbg:
        dbg_out["d_hfm"] = nc.dram_tensor("d_hfm", [128, NF * T], BF16, kind="ExternalOutput").ap()
    if "qk" in dbg:
        for nm in ("d_sk", "d_v", "d_ktl", "d_qh", "d_sq"):
            dbg_out[nm] = nc.dram_tensor(nm, [128, T if nm in ("d_sk", "d_v") else 128],
                                         BF16, kind="ExternalOutput").ap()
        for nm in ("d_lam",):
            dbg_out[nm] = nc.dram_tensor(nm, [128, 128], F32, kind="ExternalOutput").ap()
    if "o" in dbg:
        dbg_out["d_o"] = nc.dram_tensor("d_o", [NG * NT * 128, 512], BF16, kind="ExternalOutput").ap()
    if "xmid" in dbg:
        dbg_out["d_xmid"] = nc.dram_tensor("d_xmid", [T, D], F32, kind="ExternalOutput").ap()
        dbg_out["d_logits"] = nc.dram_tensor("d_logits", [128, NT * E], F32, kind="ExternalOutput").ap()
    if "comb" in dbg:
        dbg_out["d_comb"] = nc.dram_tensor("d_comb", [128, NT * E], F32, kind="ExternalOutput").ap()

    with ExitStack() as st:
        p = Prog(nc, st)
        pf = [st.enter_context(nc.psum_tensor(f"pf{i}", [128, 512], F32)) for i in range(6)]
        pb = [st.enter_context(nc.psum_tensor(f"pb{i}", [128, 1024], BF16)) for i in range(2)]

        stA = ExitStack()
        c_idb = sb(st, nc, "c_idb", [128, 128], BF16)
        c_idf = sb(st, nc, "c_idf", [128, 128], F32)
        c_tri = sb(st, nc, "c_tri", [128, 128], F32)
        c_mkeep = sb(st, nc, "c_mkeep", [128, 128], F32)
        c_mlow = sb(st, nc, "c_mlow", [128, 128], BF16)
        c_u1 = sb(st, nc, "c_u1", [128, 128], BF16)
        c_u2 = sb(st, nc, "c_u2", [128, 128], BF16)
        c_iota = sb(st, nc, "c_iota", [128, E], F32)
        c_vals = sb(st, nc, "c_vals", [128, 4], F32)
        fence_t = sb(st, nc, "fence_t", [128, 128], BF16)
        c_cos = sb(stA, nc, "c_cos", [128, NT * 64], F32)
        c_sin = sb(stA, nc, "c_sin", [128, NT * 64], F32)
        for dst, src in ((c_idb, id_bf), (c_idf, id_f), (c_tri, tri16), (c_mkeep, mkeep),
                         (c_mlow, mlow), (c_u1, u1c), (c_u2, u2c),
                         (c_cos, cos_t), (c_sin, sin_t), (c_iota, iota8),
                         (c_vals, cvals),):
            p.qs.dma(dst[:], src)
        t_const = ("q", p.qs.sem, p.qs.n)
        p.qs.wait(t_const)
        p.qg.dma(wqb, wq)
        p.qg.dma(wkb, wk)
        p.qg.dma(wvb, wv)
        p.qg.dma(wob, wo)
        p.qg.dma(wrtb, wrt)
        p.qg.dma(fence_t[:], wob[D - 128:D, 0:128])
        t_wcq = ("q", p.qg.sem, p.qg.n)
        p.qg.wait(t_wcq)
        for e_ in range(E):
            p.qg.dma(wgb[e_, :, :], wg_s[e_, :, :])
            p.qg.dma(wub[e_, :, :], wu_s[e_, :, :])
            p.qg.dma(wdbs[e_, :, :], wd_s[e_, :, :])
        p.qg.dma(fence_t[:], wdbs[E - 1, IFF - 128:IFF, 0:128])
        t_wcf = ("q", p.qg.sem, p.qg.n)
        nc.const_aps.register(F32, EPS, c_vals[:, 0:1])
        nc.const_aps.register(F32, 0.5, c_vals[:, 1:2])
        nc.const_aps.register(F32, -0.5, c_vals[:, 2:3])
        nc.const_aps.register(F32, ISQD, c_vals[:, 3:4])

        mark("p1")
        h_fm = sb(stA, nc, "h_fm", [128, NF * T], BF16)
        pbq_free = [None, None]
        with ExitStack() as st1:
            xbuf = [sb(st1, nc, f"xbuf{i}", [128, D], F32, side="right") for i in range(4)]
            htok = [sb(st1, nc, f"htok{i}", [128, D], BF16, side="right") for i in range(4)]
            scrD = sb(st1, nc, "scrD", [128, D], F32, side="right")
            stats = [sb(st1, nc, f"stats{i}", [128, 4], F32, side="right") for i in range(4)]
            rstds = [sb(st1, nc, f"rstd{i}", [128, 1], F32, side="right") for i in range(4)]
            xb_free = [[] for _ in range(4)]
            ht_free = [[] for _ in range(4)]
            s1_free = [[] for _ in range(4)]
            for it in range(NT):
                bi = it % 4
                s1 = stats[bi]
                rstd = rstds[bi]
                for tk in xb_free[bi]:
                    p.qs.wait(tk)
                tx = p.qs.dma(xbuf[bi][:], x_in[it * 128:(it + 1) * 128, :])
                p.dop("reduce_sum", [tx] + s1_free[bi], s1[:, 0:1], xbuf[bi][:], axis=AX)
                t0 = p.dve.tok()
                p.dop("tensor_scalar", [t0], s1[:, 2:3], s1[:, 0:1], -1.0 / D, None, ALU.mult)
                tmu = p.dve.tok()
                p.aop([tmu], scrD[:], xbuf[bi][:], AF.Square, bias=s1[:, 2:3], scale=1.0,
                      accum_out=s1[:, 1:2])
                tsq = p.act.tok()
                p.dop("tensor_scalar", [tsq], s1[:, 3:4], s1[:, 1:2], 1.0 / D, None, ALU.mult)
                tvar = p.dve.tok()
                p.aop([tvar], rstd[:], s1[:, 3:4], AF.Ln, bias=EPS, scale=1.0)
                tln = p.act.tok()
                p.aop([tln], rstd[:], rstd[:], AF.Exp, bias=0.0, scale=-0.5)
                trs = p.act.tok()
                p.dop("tensor_scalar", [trs, tmu] + ht_free[bi], htok[bi][:], xbuf[bi][:],
                      s1[:, 2:3], rstd[:], ALU.add, ALU.mult)
                th = p.dve.tok()
                xb_free[bi] = [th, tsq]
                s1_free[bi] = [trs, th]
                for fc in range(NF):
                    bk = fc % 2
                    p.transp([th, t_const, pbq_free[bk]], pb[bk][:, 0:128],
                             htok[bi][:, fc * 128:(fc + 1) * 128], c_idb[:])
                    tt = p.pe.tok()
                    p.dop("tensor_copy", [tt], h_fm[:, fc * T + it * 128: fc * T + (it + 1) * 128],
                          pb[bk][:, 0:128])
                    pbq_free[bk] = p.dve.tok()
                ht_free[bi] = [p.pe.tok()]
            if "d_hfm" in dbg_out:
                p.qs.wait(p.dve.tok())
                p.qs.dma(dbg_out["d_hfm"], h_fm[:])

        p.barrier(skip_qg=True)


        mark("p2")
        k_scr = nc.dram_tensor("k_scr", [NG, 128, T], F32, kind="Internal").ap()
        q_scr = nc.dram_tensor("q_scr", [NG, 4, 128, T], F32, kind="Internal").ap()
        v_scr = nc.dram_tensor("v_scr", [NG, NT, 128, 128], BF16, kind="Internal").ap()
        with ExitStack() as st2:
            wq_g = [sbr(st2, nc, f"wq_g{j}", [128, NF * 512], BF16) for j in range(2)]
            wk_g = [sbr(st2, nc, f"wk_g{j}", [128, NF * 128], BF16) for j in range(2)]
            wv_g = [sbr(st2, nc, f"wv_g{j}", [128, NF * 128], BF16) for j in range(2)]
            qstg = [sbr(st2, nc, f"qstg{j}", [128, 2048], F32) for j in range(2)]
            kstg = [sbr(st2, nc, f"kstg{j}", [128, 512], F32) for j in range(2)]
            vstg = [sbr(st2, nc, f"vstg{j}", [128, 512], BF16) for j in range(2)]
            gdw = {}
            f_stg = {}
            f_bank = {}

            def load_group(g_):
                gb = g_ % 2
                tkw = gdw.get(f"wgrp{gb}")
                if tkw is not None:
                    p.qs.wait(tkw)
                p.qs.wait(t_wcq)
                nh2 = NF // 2
                for h_ in range(2):
                    p.qs.dma(wq_g[gb][:, h_ * nh2 * 512:(h_ + 1) * nh2 * 512]
                             .rearrange("p (c f) -> p c f", c=nh2),
                             wqb[h_ * nh2 * 128:(h_ + 1) * nh2 * 128,
                                 g_ * 512:(g_ + 1) * 512].rearrange("(c p) f -> p c f", p=128))
                p.qs.dma(wk_g[gb][:].rearrange("p (c f) -> p c f", c=NF),
                         wkb[:, g_ * 128:(g_ + 1) * 128].rearrange("(c p) f -> p c f", p=128))
                p.qs.dma(wv_g[gb][:].rearrange("p (c f) -> p c f", c=NF),
                         wvb[:, g_ * 128:(g_ + 1) * 128].rearrange("(c p) f -> p c f", p=128))
                return ("q", p.qs.sem, p.qs.n)

            tw = load_group(0)
            for g in range(NG):
                gb = g % 2
                if g + 1 < NG:
                    tw_next = load_group(g + 1)
                for s in range(4):
                    sp = s % 2
                    csl = slice(s * 512, (s + 1) * 512)
                    for hh in range(4):
                        for fc in range(NF):
                            p.mm(([tw] + f_bank.get(hh, [])) if fc == 0 else [],
                                 pf[hh][:, 0:512],
                                 wq_g[gb][:, fc * 512 + hh * 128: fc * 512 + (hh + 1) * 128],
                                 h_fm[:, fc * T + s * 512: fc * T + (s + 1) * 512],
                                 start=(fc == 0), stop=(fc == NF - 1))
                        t_mm = p.pe.tok()
                        p.aop([t_mm] + (f_stg.get(f"q{sp}", []) if hh == 0 else []),
                              qstg[sp][:, hh * 512:(hh + 1) * 512], pf[hh][:, 0:512], AF.Copy)
                        f_bank[hh] = [p.act.tok()]
                    t_qe = p.act.tok()
                    p.qv.wait(t_qe)
                    for hh in range(4):
                        p.qv.dma(q_scr[g, hh][:, csl], qstg[sp][:, hh * 512:(hh + 1) * 512])
                    f_stg[f"q{sp}"] = [("q", p.qv.sem, p.qv.n)]
                    for fc in range(NF):
                        p.mm(([tw] + f_bank.get(4, [])) if fc == 0 else [], pf[4][:, 0:512],
                             wk_g[gb][:, fc * 128:(fc + 1) * 128],
                             h_fm[:, fc * T + s * 512: fc * T + (s + 1) * 512],
                             start=(fc == 0), stop=(fc == NF - 1))
                    t_km = p.pe.tok()
                    p.aop([t_km] + f_stg.get(f"k{sp}", []), kstg[sp][:], pf[4][:, 0:512], AF.Copy)
                    t_ke = p.act.tok()
                    f_bank[4] = [t_ke]
                    p.qv.wait(t_ke)
                    p.qv.dma(k_scr[g][:, csl], kstg[sp][:])
                    f_stg[f"k{sp}"] = [("q", p.qv.sem, p.qv.n)]
                    for vt in range(4):
                        i_ = s * 4 + vt
                        for fc in range(NF):
                            p.mm(([tw] + f_bank.get(5, [])) if (vt == 0 and fc == 0) else [],
                                 pf[5][:, vt * 128:(vt + 1) * 128],
                                 h_fm[:, fc * T + i_ * 128: fc * T + (i_ + 1) * 128],
                                 wv_g[gb][:, fc * 128:(fc + 1) * 128],
                                 start=(fc == 0), stop=(fc == NF - 1))
                    t_vm = p.pe.tok()
                    p.dop("tensor_copy", [t_vm] + f_stg.get(f"v{sp}", []), vstg[sp][:],
                          pf[5][:, 0:512])
                    t_ve = p.dve.tok()
                    f_bank[5] = [t_ve]
                    p.qv.wait(t_ve)
                    p.qv.dma(v_scr[g, s * 4:(s + 1) * 4, :, :].rearrange("t p c -> p t c"),
                             vstg[sp][:].rearrange("p (t c) -> p t c", t=4))
                    f_stg[f"v{sp}"] = [("q", p.qv.sem, p.qv.n)]
                gdw[f"wgrp{gb}"] = p.pe.tok()
                if g + 1 < NG:
                    tw = tw_next
            t_p2 = ("q", p.qv.sem, p.qv.n)
            p2_bank = dict(f_bank)

        mark("p3")
        with ExitStack() as st3:
            VW = 132
            sk_g = [sbr(st3, nc, f"sk_g{j}", [128, T], BF16) for j in range(NG)]
            v_g = [sbr(st3, nc, f"v_g{j}", [128, NT * VW], BF16) for j in range(NG)]
            s_f = [sbr(st3, nc, f"s_f{j}", [128, 128], F32) for j in range(NG)]
            s_b = [sbr(st3, nc, f"s_b{j}", [128, 128], BF16) for j in range(NG)]
            lamC = [sbr(st3, nc, f"lamC{j}", [128, 1], F32) for j in range(3)]
            kf32 = [sbr(st3, nc, f"kf32_{j}", [128, 128], F32) for j in range(4)]
            q4f = [sbr(st3, nc, f"q4f_{j}", [128, 512], F32) for j in range(4)]
            ktok = [sbr(st3, nc, f"ktok{j}", [128, 128], F32) for j in range(2)]
            uscr = [sbr(st3, nc, f"uscr{j}", [128, 128], F32) for j in range(2)]
            wscr = [sbr(st3, nc, f"wscr{j}", [128, 128], F32) for j in range(2)]
            lam = [sbr(st3, nc, f"lam{j}", [128, 128], F32) for j in range(2)]
            dts = [sbr(st3, nc, f"dts{j}", [128, 128], F32) for j in range(2)]
            ek = [sbr(st3, nc, f"ek{j}", [128, 128], F32) for j in range(2)]
            ktl_b = [sbr(st3, nc, f"ktl_b{j}", [128, 128], BF16) for j in range(3)]
            ktl_f = [sbr(st3, nc, f"ktl_f{j}", [128, 128], BF16) for j in range(3)]
            kst = [sbr(st3, nc, f"kst{j}", [128, 4], F32) for j in range(2)]
            rt0 = [sbr(st3, nc, f"rt0_{j}", [128, 512], F32) for j in range(2)]
            rt1 = [sbr(st3, nc, f"rt1_{j}", [128, 512], F32) for j in range(2)]
            sq4 = [sbr(st3, nc, f"sq4_{j}", [128, 512], BF16) for j in range(3)]
            sq4t = [sbr(st3, nc, f"sq4t{j}", [128, 512], BF16) for j in range(2)]
            sktok = [sbr(st3, nc, f"sktok{j}", [128, 128], BF16) for j in range(2)]
            qtok4 = [sbr(st3, nc, f"qtok4_{j}", [128, 512], F32) for j in range(2)]
            eq4 = [sbr(st3, nc, f"eq4_{j}", [128, 512], F32) for j in range(2)]
            qh4b = [sbr(st3, nc, f"qh4b{j}", [128, 512], BF16) for j in range(2)]
            qh4f = [sbr(st3, nc, f"qh4f{j}", [128, 512], BF16) for j in range(3)]
            qst = [sbr(st3, nc, f"qst{j}", [128, 16], F32) for j in range(3)]
            pstr = [sbr(st3, nc, f"pstr{j}", [128, T], BF16) for j in range(2)]
            zyp = [sbr(st3, nc, f"zyp{j}", [128, 8], F32) for j in range(2)]
            ptm = [sbr(st3, nc, f"ptm{j}", [128, 128], BF16) for j in range(2)]
            stmp = sbr(st3, nc, "stmp", [128, 128], F32)
            ostg = [sbr(st3, nc, f"ostg{j}", [128, 512], BF16) for j in range(2)]
            o_sb = [sbr(st3, nc, f"o_sb{j}", [128, 128], BF16) for j in range(2)]

            NU = NT * NG
            gd = {f"pf{b}": t for b, t in p2_bank.items()}
            ld_tok = {}
            skg_tok = {}

            def G(key):
                return gd.get(key, [])

            for g in range(NG):
                p.dop("memset", [], s_f[g][:], 0.0)
                p.dop("memset", [], s_b[g][:], 0.0)
                p.dop("memset", [], v_g[g][:], 1.0)
            t_ms = p.dve.tok()
            for g in range(NG):
                gd[f"s_b_w{g}"] = [t_ms]
                gd[f"s_f_w{g}"] = [t_ms]

            def load_u(u):
                i, g = divmod(u, NG)
                m = u % 4
                p.qs.wait(t_p2)
                p.qs.wait(t_ms)
                for tk in G(f"kf32r{m}") + G(f"q4fr{m}"):
                    p.qs.wait(tk)
                p.qs.dma(kf32[m][:], k_scr[g][:, i * 128:(i + 1) * 128])
                p.qs.dma(q4f[m][:].rearrange("p (h c) -> p h c", h=4),
                         q_scr[g, :, :, i * 128:(i + 1) * 128].rearrange("h p c -> p h c"))
                p.qs.dma(v_g[g][:, i * VW:i * VW + 128], v_scr[g, i, :, :])
                ld_tok[u] = ("q", p.qs.sem, p.qs.n)

            def tail(u):
                i, g = divmod(u, NG)
                po = u % 3
                pi = u % 2
                m = u % 4
                csl = slice(i * 128, (i + 1) * 128)
                c64 = slice(i * 64, (i + 1) * 64)
                t_ld = ld_tok[u]
                p.transp([t_ld] + G("pf2"), pf[2][:, 0:128], kf32[m][:], c_idf[:])
                t_kt = p.pe.tok()
                p.aop([t_kt] + G(f"ktok{pi}"), ktok[pi][:], pf[2][:, 0:128], AF.Copy)
                t_ktok = p.act.tok()
                gd["pf2"] = [t_ktok]
                gd[f"kf32r{m}"] = [t_kt]
                ktv = ktok[pi][:].rearrange("p (c f) -> p c f", c=2)
                cosb = c_cos[:, c64].unsqueeze(1).broadcast_to([128, 2, 64])
                sinb = c_sin[:, c64].unsqueeze(1).broadcast_to([128, 2, 64])
                p.pop("tensor_tensor", [t_ktok] + G(f"rt{pi}"),
                      rt0[pi][:, 0:128].rearrange("p (c f) -> p c f", c=2), ktv, cosb, ALU.mult)
                p.pop("tensor_tensor", [],
                      rt1[pi][:, 0:128].rearrange("p (c f) -> p c f", c=2), ktv, sinb, ALU.mult)
                t_r = p.pool.tok()
                p.pop("tensor_sub", [t_r] + G(f"sktok{pi}"), sktok[pi][:, 0:64],
                      rt0[pi][:, 0:64], rt1[pi][:, 64:128])
                p.pop("tensor_add", [], sktok[pi][:, 64:128], rt0[pi][:, 64:128], rt1[pi][:, 0:64])
                t_skt = p.pool.tok()
                p.transp([t_skt, pbq_free[1]], pb[1][:, 0:128], sktok[pi][:], c_idb[:])
                t_skT = p.pe.tok()
                gd[f"sktok{pi}"] = [t_skT]
                p.dop("tensor_copy", [t_skT], sk_g[g][:, csl], pb[1][:, 0:128])
                t_skg = p.dve.tok()
                pbq_free[1] = t_skg
                skg_tok[u] = t_skg
                p.aop([t_ktok] + G(f"uscr{pi}"), uscr[pi][:], ktok[pi][:], AF.Exp, scale=-1.0)
                t_u = p.act.tok()
                p.aop([t_u] + G(f"wscr{pi}"), wscr[pi][:], uscr[pi][:], AF.Ln, bias=1.0)
                t_w = p.act.tok()
                gd[f"uscr{pi}"] = [t_w]
                p.dop("reduce_max", [t_ktok], kst[pi][:, 0:1], ktok[pi][:], axis=AX, negate=True)
                t_nm = p.dve.tok()
                p.aop([t_nm] + G(f"ek{pi}"), ek[pi][:], ktok[pi][:], AF.Exp, bias=kst[pi][:, 0:1],
                      accum_out=kst[pi][:, 1:2])
                t_ek = p.act.tok()
                gd[f"ktok{pi}"] = [t_ek, t_nm]
                p.dop("reciprocal", [t_ek], kst[pi][:, 2:3], kst[pi][:, 1:2])
                t_rk = p.dve.tok()
                p.mm([t_w] + G("pf0"), pf[0][:, 0:128], c_tri[:], wscr[pi][:], start=True, stop=True)
                p.mm([], pf[0][:, 128:256], wscr[pi][:], c_tri[:], start=True, stop=True)
                t_At = p.pe.tok()
                gd[f"wscr{pi}"] = [t_At]
                p.aop([t_At] + G(f"lam{pi}"), lam[pi][:], pf[0][:, 0:128], AF.Exp)
                t_lam = p.act.tok()
                p.aop(G(f"dts{pi}"), dts[pi][:], pf[0][:, 0:128], AF.Exp, scale=-1.0)
                t_dts = p.act.tok()
                p.aop([t_At] + G(f"lamC{po}"), lamC[po][:], pf[0][:, 255:256], AF.Exp)
                t_lamC = p.act.tok()
                gd["pf0"] = [t_lamC, t_dts]
                gd[f"lamC{po}_w"] = [t_lamC]
                p.dop("scalar_tensor_tensor", [t_rk, t_dts] + G(f"ktlb{po}"),
                      ktl_b[po][:], ek[pi][:], kst[pi][:, 2:3], dts[pi][:], ALU.mult, ALU.mult)
                t_ktl = p.dve.tok()
                gd[f"ek{pi}"] = [t_ktl]
                gd[f"dts{pi}"] = [t_ktl]
                gd[f"ktlb{po}_w"] = [t_ktl]
                p.transp([t_ktl, pbq_free[0]], pb[0][:, 0:128], ktl_b[po][:], c_idb[:])
                t_ktlT = p.pe.tok()
                p.dop("tensor_copy", [t_ktlT] + G(f"ktlf{po}"), ktl_f[po][:], pb[0][:, 0:128])
                t_ktlf = p.dve.tok()
                pbq_free[0] = t_ktlf
                gd[f"ktlf{po}_w"] = [t_ktlf]
                tq_ev = []
                for hh in range(4):
                    bkq = 2 + (hh % 2)
                    p.transp([t_ld] + G(f"pf{bkq}"), pf[bkq][:, 0:128],
                             q4f[m][:, hh * 128:(hh + 1) * 128], c_idf[:])
                    t_tt = p.pe.tok()
                    p.aop([t_tt] + (G(f"qtok4_{pi}") if hh == 0 else []),
                          qtok4[pi][:, hh * 128:(hh + 1) * 128], pf[bkq][:, 0:128], AF.Copy)
                    t_ev = p.act.tok()
                    gd[f"pf{bkq}"] = [t_ev]
                    tq_ev.append(t_ev)
                gd[f"q4fr{m}"] = [p.pe.tok()]
                q4vv = qtok4[pi][:].rearrange("p (r c f) -> p r c f", r=4, c=2)
                cosb4 = c_cos[:, c64].unsqueeze(1).unsqueeze(1).broadcast_to([128, 4, 2, 64])
                sinb4 = c_sin[:, c64].unsqueeze(1).unsqueeze(1).broadcast_to([128, 4, 2, 64])
                p.pop("tensor_tensor", [tq_ev[-1], t_skt] + G(f"rt{pi}"),
                      rt0[pi][:].rearrange("p (r c f) -> p r c f", r=4, c=2), q4vv, cosb4, ALU.mult)
                p.pop("tensor_tensor", [],
                      rt1[pi][:].rearrange("p (r c f) -> p r c f", r=4, c=2), q4vv, sinb4, ALU.mult)
                t_qr = p.pool.tok()
                r0v = rt0[pi][:].rearrange("p (r c f) -> p r c f", r=4, c=2)
                r1v = rt1[pi][:].rearrange("p (r c f) -> p r c f", r=4, c=2)
                s4v = sq4t[pi][:].rearrange("p (r c f) -> p r c f", r=4, c=2)
                p.pop("tensor_tensor", [t_qr] + G(f"sq4t{pi}"), s4v[:, :, 0, :], r0v[:, :, 0, :],
                      r1v[:, :, 1, :], ALU.subtract)
                p.pop("tensor_tensor", [], s4v[:, :, 1, :], r0v[:, :, 1, :],
                      r1v[:, :, 0, :], ALU.add)
                t_sq4t = p.pool.tok()
                gd[f"rt{pi}"] = [t_sq4t]
                for hh in range(4):
                    bkb = hh % 2
                    p.transp([t_sq4t, pbq_free[bkb]], pb[bkb][:, 0:128],
                             sq4t[pi][:, hh * 128:(hh + 1) * 128], c_idb[:])
                    t_tt = p.pe.tok()
                    p.dop("tensor_copy", [t_tt] + (G(f"sq4_{po}") if hh == 0 else []),
                          sq4[po][:, hh * 128:(hh + 1) * 128], pb[bkb][:, 0:128])
                    pbq_free[bkb] = p.dve.tok()
                t_sq4v = p.dve.tok()
                gd[f"sq4t{pi}"] = [p.pe.tok()]
                gd[f"sq4_{po}_w"] = [t_sq4v]
                p.dop("reduce_max", [tq_ev[-1]], qst[po][:, 0:4],
                      qtok4[pi][:].rearrange("p (r f) -> p r f", r=4), axis=AX, negate=True)
                t_qnm = p.dve.tok()
                for hh in range(4):
                    p.aop([t_qnm] + (G(f"eq4_{pi}") if hh == 0 else []),
                          eq4[pi][:, hh * 128:(hh + 1) * 128], qtok4[pi][:, hh * 128:(hh + 1) * 128],
                          AF.Exp, bias=qst[po][:, hh:hh + 1], accum_out=qst[po][:, 4 + hh:5 + hh])
                t_eq = p.act.tok()
                gd[f"qtok4_{pi}"] = [t_eq, t_qnm]
                p.dop("reciprocal", [t_eq], qst[po][:, 8:12], qst[po][:, 4:8])
                t_rq = p.dve.tok()
                p.dop("tensor_scalar", [t_rq], qst[po][:, 12:16], qst[po][:, 8:12], 0.5, None,
                      ALU.mult)
                t_rq5 = p.dve.tok()
                gd[f"rq5_{po}"] = [t_rq5]
                p.pop("tensor_tensor", [t_eq, t_lam] + G(f"qh4b{pi}"),
                      qh4b[pi][:].rearrange("p (r f) -> p r f", r=4),
                      eq4[pi][:].rearrange("p (r f) -> p r f", r=4),
                      lam[pi][:].unsqueeze(1).broadcast_to([128, 4, 128]), ALU.mult)
                t_qhb = p.pool.tok()
                gd[f"eq4_{pi}"] = [t_qhb]
                gd[f"lam{pi}"] = [t_qhb]
                for hh in range(4):
                    bkb = hh % 2
                    p.transp([t_qhb, pbq_free[bkb]], pb[bkb][:, 0:128],
                             qh4b[pi][:, hh * 128:(hh + 1) * 128], c_idb[:])
                    t_tt = p.pe.tok()
                    p.dop("tensor_copy", [t_tt] + (G(f"qh4f{po}") if hh == 0 else []),
                          qh4f[po][:, hh * 128:(hh + 1) * 128], pb[bkb][:, 0:128])
                    pbq_free[bkb] = p.dve.tok()
                gd[f"qh4f{po}_w"] = [p.dve.tok()]
                gd[f"qh4b{pi}"] = [p.pe.tok()]

            def scores(u, hh, bank):
                i, g = divmod(u, NG)
                po = u % 3
                hp = hh % 2
                nkb = i // 4 + 1
                for kb in range(nkb):
                    j0 = kb * 4
                    j1 = min(i + 1, j0 + 4)
                    for j in range(j0, j1):
                        p.mm(G(f"sq4_{po}_w") + [skg_tok[u]] + G(f"pf{bank}") if j == j0 else [],
                             pf[bank][:, (j - j0) * 128:(j - j0 + 1) * 128],
                             sk_g[g][:, j * 128:(j + 1) * 128],
                             sq4[po][:, hh * 128:(hh + 1) * 128],
                             start=True, stop=(j < i))
                        if j == i:
                            p.mm([t_const], pf[bank][:, (j - j0) * 128:(j - j0 + 1) * 128],
                                 c_u2[:], c_u1[:], start=False, stop=True)
                    t_sc = p.pe.tok()
                    nk = (j1 - j0) * 128
                    p.aop([t_sc] + (G(f"pstr{hp}") if kb == 0 else []),
                          pstr[hp][:, j0 * 128: j0 * 128 + nk], pf[bank][:, 0:nk], AF.Exp,
                          scale=ISQD)
                    t_pex = p.act.tok()
                    gd[f"pf{bank}"] = [t_pex]
                gd[f"pexp{hp}"] = [t_pex]

            def headrest(u, hh):
                i, g = divmod(u, NG)
                po = u % 3
                hp = hh % 2
                t_pex = G(f"pexp{hp}")[0]
                for j in range(i + 1):
                    p.mm([t_pex, ld_tok[u]] + G("pf1") if j == 0 else [], pf[1][:, 0:129],
                         pstr[hp][:, j * 128:(j + 1) * 128],
                         v_g[g][:, j * VW: j * VW + 129],
                         start=(j == 0), stop=(j == i))
                t_ype = p.pe.tok()
                gd[f"pstr{hp}"] = [t_ype]
                p.mm(G(f"ktlf{po}_w") + G(f"qh4f{po}_w") + G("pf4"), pf[4][:, 0:128],
                     ktl_f[po][:], qh4f[po][:, hh * 128:(hh + 1) * 128], start=True, stop=True)
                t_pt = p.pe.tok()
                p.dop("tensor_mul", [t_pt, t_const] + G(f"ptm{hp}"), ptm[hp][:],
                      pf[4][:, 0:128], c_mkeep[:])
                t_ptm = p.dve.tok()
                p.mm([t_ptm], pf[4][:, 128:256], ptm[hp][:],
                     v_g[g][:, i * VW: i * VW + 128], start=True, stop=False)
                p.mm(G(f"s_b_w{g}"), pf[4][:, 128:256], qh4f[po][:, hh * 128:(hh + 1) * 128],
                     s_b[g][:], start=False, stop=True)
                t_gpe = p.pe.tok()
                gd[f"ptm{hp}"] = [t_gpe]
                p.dop("reciprocal", [t_ype] + G(f"rzy{hp}"), zyp[hp][:, 0:1], pf[1][:, 128:129])
                t_rz = p.dve.tok()
                p.dop("tensor_scalar", [t_rz] + G(f"osb{hp}"), o_sb[hp][:], pf[1][:, 0:128],
                      zyp[hp][:, 0:1], 0.5, ALU.mult, ALU.mult)
                t_o1 = p.dve.tok()
                gd["pf1"] = [t_o1]
                gd[f"rzy{hp}"] = [t_o1]
                p.dop("scalar_tensor_tensor", [t_o1, t_gpe] + G(f"rq5_{po}"), o_sb[hp][:],
                      pf[4][:, 128:256], qst[po][:, 12 + hh:13 + hh], o_sb[hp][:],
                      ALU.mult, ALU.add)
                t_o2 = p.dve.tok()
                gd["pf4"] = [t_o2]
                p.transp([t_o2, pbq_free[hp]], pb[hp][:, 0:128], o_sb[hp][:], c_idb[:])
                t_oT = p.pe.tok()
                gd[f"osb{hp}"] = [t_oT]
                p.dop("tensor_copy", [t_oT] + G(f"ostg_dma{u % 2}"),
                      ostg[u % 2][:, hh * 128:(hh + 1) * 128], pb[hp][:, 0:128])
                pbq_free[hp] = p.dve.tok()

            def state(u):
                i, g = divmod(u, NG)
                po = u % 3
                p.mm(G(f"ktlb{po}_w") + [ld_tok[u]] + G("pf3"), pf[3][:, 0:128],
                     ktl_b[po][:], v_g[g][:, i * VW:i * VW + 128], start=True, stop=True)
                t_kv = p.pe.tok()
                gd[f"ktlb{po}"] = [t_kv]
                p.dop("tensor_add", [t_kv] + G(f"s_f_w{g}") + G("stmp"), stmp[:],
                      pf[3][:, 0:128], s_f[g][:])
                t_st = p.dve.tok()
                gd["pf3"] = [t_st]
                p.dop("tensor_scalar", [t_st] + G(f"lamC{po}_w"), s_f[g][:], stmp[:],
                      lamC[po][:], None, ALU.mult)
                t_sf = p.dve.tok()
                gd[f"s_f_w{g}"] = [t_sf]
                gd["stmp"] = [t_sf]
                gd[f"lamC{po}"] = [t_sf]
                p.aop([t_sf, p.pe.tok()], s_b[g][:], s_f[g][:], AF.Copy)
                gd[f"s_b_w{g}"] = [p.act.tok()]
                gd[f"ktlf{po}"] = [p.pe.tok()]

            load_u(0)
            load_u(1)
            load_u(2)
            load_u(3)
            tail(0)
            tail(1)
            for u in range(NU):
                i, g = divmod(u, NG)
                scores(u, 0, 5)
                scores(u, 1, 2)
                if u + 4 < NU:
                    load_u(u + 4)
                headrest(u, 0)
                scores(u, 2, 5)
                headrest(u, 1)
                scores(u, 3, 2)
                headrest(u, 2)
                headrest(u, 3)
                gd[f"sq4_{u % 3}"] = [p.pe.tok()]
                gd[f"qh4f{u % 3}"] = [p.pe.tok()]
                t_oe = p.dve.tok()
                p.qs.wait(t_oe)
                t_od = p.qs.dma(o_scr[g, i, :, :], ostg[u % 2][:])
                gd[f"ostg_dma{u % 2}"] = [t_od]
                if u + 2 < NU:
                    tail(u + 2)
                state(u)

        p.barrier()

        if "d_o" in dbg_out:
            p.qs.dma(dbg_out["d_o"], o_scr.rearrange("g i p c -> (g i p) c"))
            p.barrier()
        f_pf0 = f_pf1 = f_pf2 = f_pf3 = f_pf4 = f_pf5 = None

        mark("p4")
        stA.close()
        stB = ExitStack()
        stB4 = ExitStack()
        x_mid = sb(stB, nc, "x_mid", [128, NTH * D], F32)
        comb_tok = sb(stB, nc, "comb_tok", [128, NT * E], F32)
        h2s = sb(stB, nc, "h2s", [128, NF * TH], BF16)
        h2h = [h2s, h2s]
        wo_full = sbr(stB4, nc, "wo_full", [128, NF * D], BF16)
        wrt_sb = sbr(stB4, nc, "wrt_sb", [128, NF * E], BF16)
        o_t = [sbr(stB4, nc, f"o_t{i}", [128, D], BF16) for i in range(2)]
        xb4 = [sbr(stB4, nc, f"xb4_{i}", [128, D], F32) for i in range(2)]
        h2tok = sbr(stB4, nc, "h2tok", [128, D], BF16)
        logits_sb = sbr(stB4, nc, "logits_sb", [128, NT * E], F32)
        xsum = sbr(stB4, nc, "xsum", [128, NT * 4], F32)
        lstat = sbr(stB4, nc, "lstat", [128, 8], F32)

        p.qs.wait(t_wcq)
        for fc in range(NF):
            p.qs.dma(wrt_sb[:, fc * E:(fc + 1) * E], wrtb[fc * 128:(fc + 1) * 128, :])
        for fc in range(NF):
            p.qs.dma(wo_full[:, fc * D:(fc + 1) * D], wob[fc * 128:(fc + 1) * 128, :])
        t_wo = ("q", p.qs.sem, p.qs.n)

        f_ot = [None, None]
        f_xb = [None, None]
        f_h2t = []
        f_pfy = [None, None]
        for half in range(2):
            hbase = half * NTH
            for tt in range(NTH):
                gt = hbase + tt
                bi = tt % 2
                if f_ot[bi] is not None:
                    p.qg.wait(f_ot[bi])
                for g in range(NG):
                    p.qg.dma(o_t[bi][:, g * 512:(g + 1) * 512], o_scr[g, gt, :, :])
                if f_xb[bi] is not None:
                    p.qg.wait(f_xb[bi])
                tx = p.qg.dma(xb4[bi][:], x_in[gt * 128:(gt + 1) * 128, :])
                t_ld = tx
                for oc in range(4):
                    bk = 4 + oc % 2
                    for h in range(NH):
                        p.mm([t_wo, t_ld, f_pfy[oc % 2]] if h == 0 else [], pf[bk][:, 0:512],
                             o_t[bi][:, h * 128:(h + 1) * 128],
                             wo_full[:, h * D + oc * 512: h * D + (oc + 1) * 512],
                             start=(h == 0), stop=(h == NH - 1))
                    t_mm = p.pe.tok()
                    p.dop("scalar_tensor_tensor", [t_mm, tx],
                          x_mid[:, tt * D + oc * 512: tt * D + (oc + 1) * 512],
                          pf[bk][:, 0:512], 1.0, xb4[bi][:, oc * 512:(oc + 1) * 512],
                          ALU.mult, ALU.add,
                          accum_out=xsum[:, gt * 4 + oc: gt * 4 + oc + 1])
                    f_pfy[oc % 2] = p.dve.tok()
                f_ot[bi] = p.pe.tok()
                t_xm = p.dve.tok()
                f_xb[bi] = t_xm
                if half == 0:
                    p.qg.wait(t_xm)
                    p.qg.dma(xmid_hbm[tt * 128:(tt + 1) * 128, :], x_mid[:, tt * D:(tt + 1) * D])
                p.dop("reduce_sum", [t_xm], lstat[:, 0:1], xsum[:, gt * 4:(gt + 1) * 4], axis=AX)
                t0 = p.dve.tok()
                p.dop("tensor_scalar", [t0], lstat[:, 1:2], lstat[:, 0:1], -1.0 / D, None, ALU.mult)
                t1 = p.dve.tok()
                p.aop([t1] + f_h2t, h2tok[:], x_mid[:, tt * D:(tt + 1) * D], AF.Square,
                      bias=lstat[:, 1:2], accum_out=lstat[:, 2:3])
                t2 = p.act.tok()
                p.dop("tensor_scalar", [t2], lstat[:, 3:4], lstat[:, 2:3], 1.0 / D, None, ALU.mult)
                t3 = p.dve.tok()
                p.aop([t3], lstat[:, 4:5], lstat[:, 3:4], AF.Ln, bias=EPS)
                t4 = p.act.tok()
                p.aop([t4], lstat[:, 5:6], lstat[:, 4:5], AF.Exp, scale=-0.5)
                t5 = p.act.tok()
                p.dop("tensor_scalar", [t5], h2tok[:], x_mid[:, tt * D:(tt + 1) * D],
                      lstat[:, 1:2], lstat[:, 5:6], ALU.add, ALU.mult)
                t_h2 = p.dve.tok()
                for fc in range(NF):
                    bkb = fc % 2
                    p.transp([t_h2, pbq_free[bkb]], pb[bkb][:, 0:128],
                             h2tok[:, fc * 128:(fc + 1) * 128], c_idb[:])
                    t_tt2 = p.pe.tok()
                    p.dop("tensor_copy", [t_tt2],
                          h2h[half][:, fc * TH + tt * 128: fc * TH + (tt + 1) * 128],
                          pb[bkb][:, 0:128])
                    pbq_free[bkb] = p.dve.tok()
                t_h2f = p.dve.tok()
                f_h2t = [p.pe.tok()]
                for fc in range(NF):
                    p.mm([t_h2f, t_const, f_pf3] if fc == 0 else [], pf[3][:, 0:E],
                         h2h[half][:, fc * TH + tt * 128: fc * TH + (tt + 1) * 128],
                         wrt_sb[:, fc * E:(fc + 1) * E], start=(fc == 0), stop=(fc == NF - 1))
                t_lg = p.pe.tok()
                p.dop("tensor_copy", [t_lg], logits_sb[:, gt * E:(gt + 1) * E], pf[3][:, 0:E])
                f_pf3 = p.dve.tok()
            if half == 0:
                th2 = p.dve.tok()
                p.qs.wait(th2)
                th2s = p.qs.dma(h2_hbm, h2s[:])
                p.dve.wait(th2s)

        if "d_xmid" in dbg_out:
            p.barrier()
            p.qs.dma(dbg_out["d_xmid"][0:TH, :], xmid_hbm[:, :])
            for tt in range(NTH):
                p.qs.dma(dbg_out["d_xmid"][TH + tt * 128:TH + (tt + 1) * 128, :],
                         x_mid[:, tt * D:(tt + 1) * D])
            p.qs.dma(dbg_out["d_logits"], logits_sb[:])
        p.barrier()

        if upto != "p4":
            mark("pc")
            with ExitStack() as stPC:
                gate = sbr(stPC, nc, "gate", [128, E], F32)
                m8 = sbr(stPC, nc, "m8", [128, E], F32)
                idxu = sbr(stPC, nc, "idxu", [128, E], mybir.dt.uint32)
                idxf = sbr(stPC, nc, "idxf", [128, E], F32)
                oh1 = sbr(stPC, nc, "oh1", [128, E], F32)
                gst = sbr(stPC, nc, "gst", [128, 4], F32)
                f_cb = []
                for gt in range(NT):
                    lsl = logits_sb[:, gt * E:(gt + 1) * E]
                    p.dop("reduce_max", [], gst[:, 0:1], lsl, axis=AX, negate=True)
                    t0 = p.dve.tok()
                    p.aop([t0] + f_cb, gate[:], lsl, AF.Exp, bias=gst[:, 0:1],
                          accum_out=gst[:, 1:2])
                    t1 = p.act.tok()
                    p.dop("reciprocal", [t1], gst[:, 2:3], gst[:, 1:2])
                    t2 = p.dve.tok()
                    p.dop("tensor_scalar", [t2], gate[:], gate[:], gst[:, 2:3], None, ALU.mult)
                    t3 = p.dve.tok()
                    p.dop("max", [t3], m8[:], gate[:])
                    t4 = p.dve.tok()
                    p.dop("max_index", [t4], idxu[:], m8[:], gate[:])
                    t5 = p.dve.tok()
                    p.dop("tensor_copy", [t5], idxf[:], idxu[:])
                    t6 = p.dve.tok()
                    p.dop("tensor_scalar", [t6, t_const], oh1[:], c_iota[:], idxf[:, 0:1],
                          None, ALU.is_equal)
                    t7 = p.dve.tok()
                    csl8 = comb_tok[:, gt * E:(gt + 1) * E]
                    p.dop("tensor_scalar", [t7], csl8, oh1[:], m8[:, 0:1], None, ALU.mult)
                    t8 = p.dve.tok()
                    p.dop("tensor_scalar", [t8], oh1[:], c_iota[:], idxf[:, 1:2], None, ALU.is_equal)
                    t9 = p.dve.tok()
                    p.dop("scalar_tensor_tensor", [t9], csl8, oh1[:], m8[:, 1:2], csl8,
                          ALU.mult, ALU.add)
                    f_cb = [p.dve.tok()]
                if "d_comb" in dbg_out:
                    p.qs.wait(p.dve.tok())
                    p.qs.dma(dbg_out["d_comb"], comb_tok[:])
            p.barrier()
            stB4.close()

        if upto in ("all", "p5e1", "p5h", "p5bc", "p5gu", "p5none", "p5nostore"):
            mark("p5")
            with ExitStack() as st5:
                gu = sbr(st5, nc, "gu", [128, NIC * TH], BF16)
                wgq = [sbr(st5, nc, f"wgq{i}", [128, NF * 384], BF16) for i in range(2)]
                wuq = [sbr(st5, nc, f"wuq{i}", [128, NF * 384], BF16) for i in range(2)]
                wdb = [sbr(st5, nc, f"wdb{i}", [128, NIC * 512], BF16) for i in range(2)]
                sgu = [sbr(st5, nc, f"sgu{i}", [128, 512], F32) for i in range(2)]
                halves = (1, 0) if upto == "all" else (1,)
                n_exp = 0 if upto in ("p5none",) else (E if upto in ("all", "p5h") else 1)
                QD = [(0, 3), (3, 3), (6, 3), (9, 2)]
                f_wg = [("e", p.pe.sem, 0), ("e", p.pe.sem, 0)]
                f_wd = [("e", p.pe.sem, 0), ("e", p.pe.sem, 0)]

                def load_q(e_, q_):
                    b = q_ % 2
                    ic0, nch = QD[q_]
                    iw = nch * 128
                    p.qs.wait(f_wg[b])
                    p.qs.wait(t_wcf)
                    for fc in range(NF):
                        p.qs.dma(wgq[b][:, fc * iw:(fc + 1) * iw],
                                 wgb[e_, fc * 128:(fc + 1) * 128, ic0 * 128:ic0 * 128 + iw])
                    for fc in range(NF):
                        p.qs.dma(wuq[b][:, fc * iw:(fc + 1) * iw],
                                 wub[e_, fc * 128:(fc + 1) * 128, ic0 * 128:ic0 * 128 + iw])
                    return ("q", p.qs.sem, p.qs.n)

                def load_d(e_, hs_):
                    b = hs_ % 2
                    p.qs.wait(f_wd[b])
                    for ic in range(NIC):
                        p.qs.dma(wdb[b][:, ic * 512:(ic + 1) * 512],
                                 wdbs[e_, ic * 128:(ic + 1) * 128, hs_ * 512:(hs_ + 1) * 512])
                    return ("q", p.qs.sem, p.qs.n)

                for half in halves:
                    hbase = half * NTH
                    if half == 0:
                        p.barrier()
                        for tt in range(NTH):
                            p.qg.dma(x_mid[:, tt * D:(tt + 1) * D], xmid_hbm[tt * 128:(tt + 1) * 128, :])
                        p.qg.dma(h2s[:], h2_hbm)
                        t_xr = ("q", p.qg.sem, p.qg.n)
                        p.dve.wait(t_xr)
                        p.pe.wait(t_xr)
                    t_gu_rd = ("e", p.pe.sem, 0)
                    f_sg = [[], []]
                    t_xev = p.dve.tok()
                    twg = load_q(0, 0) if n_exp else None
                    twd_pend = None
                    for e in range(n_exp):
                        for q in range(4):
                            b = q % 2
                            ic0, nch = QD[q]
                            iw = nch * 128
                            tw_cur = twg
                            if q < 3:
                                twg = load_q(e, q + 1)
                            elif e + 1 < n_exp:
                                twg = load_q(e + 1, 0)
                            for icl in range(nch):
                                ic = ic0 + icl
                                for s2 in range(TH // 512):
                                    gb = 0 if (icl + s2) % 2 == 0 else 2
                                    ub = gb + 1
                                    fg = f_pf0 if gb == 0 else f_pf2
                                    fu = f_pf1 if ub == 1 else f_pf3
                                    for fc in range(NF):
                                        p.mm([tw_cur, fg] if fc == 0 else [], pf[gb][:, 0:512],
                                             wgq[b][:, fc * iw + icl * 128: fc * iw + (icl + 1) * 128],
                                             h2h[half][:, fc * TH + s2 * 512: fc * TH + (s2 + 1) * 512],
                                             start=(fc == 0), stop=(fc == NF - 1))
                                    t_g = p.pe.tok()
                                    for fc in range(NF):
                                        p.mm([fu] if fc == 0 else [], pf[ub][:, 0:512],
                                             wuq[b][:, fc * iw + icl * 128: fc * iw + (icl + 1) * 128],
                                             h2h[half][:, fc * TH + s2 * 512: fc * TH + (s2 + 1) * 512],
                                             start=(fc == 0), stop=(fc == NF - 1))
                                    t_u2 = p.pe.tok()
                                    sp = (icl + s2) % 2
                                    p.aop([t_g] + f_sg[sp], sgu[sp][:], pf[gb][:, 0:512], AF.Silu)
                                    t_sg = p.act.tok()
                                    if gb == 0:
                                        f_pf0 = t_sg
                                    else:
                                        f_pf2 = t_sg
                                    p.dop("tensor_mul", [t_sg, t_u2, t_gu_rd],
                                          gu[:, ic * TH + s2 * 512: ic * TH + (s2 + 1) * 512],
                                          sgu[sp][:], pf[ub][:, 0:512])
                                    t_gf = p.dve.tok()
                                    if ub == 1:
                                        f_pf1 = t_gf
                                    else:
                                        f_pf3 = t_gf
                                    f_sg[sp] = [t_gf]
                            f_wg[b] = p.pe.tok()
                        t_gu = p.dve.tok()
                        twd = twd_pend if twd_pend is not None else load_d(e, 0)
                        for hs in range(4):
                            b = hs % 2
                            twd_cur = twd
                            if hs < 3:
                                twd = load_d(e, hs + 1)
                            for tt in range(NTH):
                                gt = hbase + tt
                                bk = 4 + tt % 2
                                fy = f_pf4 if bk == 4 else f_pf5
                                for ic in range(NIC):
                                    p.mm([twd_cur, t_gu, fy] if ic == 0 else [], pf[bk][:, 0:512],
                                         gu[:, ic * TH + tt * 128: ic * TH + (tt + 1) * 128],
                                         wdb[b][:, ic * 512:(ic + 1) * 512],
                                         start=(ic == 0), stop=(ic == NIC - 1))
                                t_y = p.pe.tok()
                                p.dop("scalar_tensor_tensor", [t_y, t_xev],
                                      x_mid[:, tt * D + hs * 512: tt * D + (hs + 1) * 512],
                                      pf[bk][:, 0:512], comb_tok[:, gt * E + e: gt * E + e + 1],
                                      x_mid[:, tt * D + hs * 512: tt * D + (hs + 1) * 512],
                                      ALU.mult, ALU.add)
                                if bk == 4:
                                    f_pf4 = p.dve.tok()
                                else:
                                    f_pf5 = p.dve.tok()
                            f_wd[b] = p.pe.tok()
                        twd_pend = None
                        t_xev = p.dve.tok()
                        t_gu_rd = p.pe.tok()
                    p.qg.wait(t_xev)
                    for tt in range(NTH):
                        gt = hbase + tt
                        p.qg.dma(out[gt * 128:(gt + 1) * 128, :], x_mid[:, tt * D:(tt + 1) * D])

            p.barrier()


        p.barrier()
        stB.close()
        mark("end")
    return nc



def _consts():
    idf = np.eye(128, dtype=np.float32)
    idb = idf.astype(ml_dtypes.bfloat16)
    s = np.arange(128)[:, None]
    t = np.arange(128)[None, :]
    tri16 = np.where(s <= t, -1.0 / 16.0, 0.0).astype(np.float32)
    mkeep = np.where(s <= t, 1.0, 0.0).astype(np.float32)
    mlow = np.where(t <= s, 1.0, 0.0).astype(ml_dtypes.bfloat16)
    u1 = np.where(s > t, 1.0, 0.0).astype(ml_dtypes.bfloat16)
    u2 = np.where(s <= t, -500.0, 0.0).astype(ml_dtypes.bfloat16)
    inv = (1.0 / ROPE_THETA ** (np.arange(0, HD, 2, dtype=np.float32) / HD)).astype(np.float64)
    ang = np.arange(T, dtype=np.float64)[:, None] * inv[None, :]
    cos = np.cos(ang).astype(np.float32).reshape(NT, 128, 64)
    sin = np.sin(ang).astype(np.float32).reshape(NT, 128, 64)
    cos_t = np.ascontiguousarray(cos.transpose(1, 0, 2).reshape(128, NT * 64))
    sin_t = np.ascontiguousarray(sin.transpose(1, 0, 2).reshape(128, NT * 64))
    iota8 = np.tile(np.arange(E, dtype=np.float32), (128, 1))
    cvals = np.tile(np.array([EPS, 0.5, -0.5, ISQD], dtype=np.float32), (128, 1))
    return dict(id_bf=idb, id_f=idf, tri16=tri16, mkeep=mkeep, mlow=mlow, u1c=u1, u2c=u2,
                cos_t=cos_t, sin_t=sin_t, iota8=iota8, cvals=cvals)


def make_in_maps(L0_attn_hidden_states, L0_attn_Wq, L0_attn_Wk, L0_attn_Wv, L0_attn_Wo,
                 L0_ffn_Wr, L0_ffn_Wg, L0_ffn_Wu, L0_ffn_Wd):
    consts = _consts()
    base = dict(consts)
    base["wq"] = np.ascontiguousarray(L0_attn_Wq, dtype=np.float32)
    base["wk"] = np.ascontiguousarray(L0_attn_Wk, dtype=np.float32)
    base["wv"] = np.ascontiguousarray(L0_attn_Wv, dtype=np.float32)
    base["wo"] = np.ascontiguousarray(L0_attn_Wo, dtype=np.float32)
    base["wrt"] = np.ascontiguousarray(np.asarray(L0_ffn_Wr, dtype=np.float32).T)
    base["wg_s"] = np.ascontiguousarray(np.asarray(L0_ffn_Wg, dtype=np.float32).transpose(0, 2, 1))
    base["wu_s"] = np.ascontiguousarray(np.asarray(L0_ffn_Wu, dtype=np.float32).transpose(0, 2, 1))
    base["wd_s"] = np.ascontiguousarray(np.asarray(L0_ffn_Wd, dtype=np.float32).transpose(0, 2, 1))
    x = np.ascontiguousarray(L0_attn_hidden_states, dtype=np.float32)
    in_maps = []
    for c in range(8):
        m = dict(base)
        m["x"] = np.ascontiguousarray(x[c])
        in_maps.append(m)
    return in_maps


def kernel(**inputs):
    in_maps = make_in_maps(**inputs)
    nc = build_program()
    res = run_bass_kernel_spmd(nc, in_maps, core_ids=list(range(8)))
    return np.stack([np.asarray(r["out"], dtype=np.float32) for r in res.results], 0)


# version 22
# speedup vs baseline: 1.0766x; 1.0766x over previous
import math
from contextlib import ExitStack

import numpy as np
import ml_dtypes

import concourse.bass as bass
import concourse.mybir as mybir
from concourse.bass_utils import run_bass_kernel_spmd

F32 = mybir.dt.float32
BF16 = mybir.dt.bfloat16
AF = mybir.ActivationFunctionType
ALU = mybir.AluOpType
AX = mybir.AxisListType.X

T = 2048
D = 2048
NT = T // 128
NF = D // 128
NH = 16
NKV = 4
HD = 128
NG = 4
E = 8
IFF = 1408
NIC = IFF // 128
CAP = 5120
EPS = 1e-5
ISQD = 1.0 / math.sqrt(HD)
ROPE_THETA = 10000.0
TH = T // 2
NTH = NT // 2


class Q:

    def __init__(self, nc, eng, sem):
        self.eng = eng
        self.sem = sem
        self.n = 0
        self.waited = {}

    def dma(self, out, in_, **kw):
        i = self.eng.dma_start(out=out, in_=in_, **kw)
        self.n += 16
        i.then_inc(self.sem, 16)
        return ("q", self.sem, self.n)

    def wait(self, tok):
        _wait(self.eng, self.waited, tok)


def _wait(eng, waited, tok):
    if tok is None:
        return
    kind, sem, n = tok
    if n <= 0:
        return
    key = id(sem)
    if waited.get(key, 0) >= n:
        return
    eng.wait_ge(sem, n)
    waited[key] = n


class Eng:

    def __init__(self, obj, sem):
        self.obj = obj
        self.sem = sem
        self.n = 0
        self.last = None
        self.last_tok = None
        self.waited = {}

    def rec(self, instr):
        self.last = instr
        self.last_tok = None
        return instr

    def tok(self):
        if self.last is None:
            return ("e", self.sem, 0)
        if self.last_tok is None:
            self.n += 1
            self.last.then_inc(self.sem, 1)
            self.last_tok = self.n
        return ("e", self.sem, self.last_tok)

    def wait(self, tok):
        _wait(self.obj, self.waited, tok)


class Prog:
    def __init__(self, nc, st):
        self.nc = nc
        self.st = st
        sems = {k: st.enter_context(nc.semaphore("s_" + k))
                for k in ("pe", "dve", "act", "pool", "qs", "qg", "qv")}
        self.pe = Eng(nc.tensor, sems["pe"])
        self.dve = Eng(nc.vector, sems["dve"])
        self.act = Eng(nc.scalar, sems["act"])
        self.pool = Eng(nc.gpsimd, sems["pool"])
        self.qs = Q(nc, nc.sync, sems["qs"])
        self.qg = Q(nc, nc.gpsimd, sems["qg"])
        self.qv = Q(nc, nc.scalar, sems["qv"])
        self.engs = [self.pe, self.dve, self.act, self.pool]

    def mm(self, eng_waits, out, lhsT, rhs, start, stop):
        for w in eng_waits:
            self.pe.wait(w)
        return self.pe.rec(self.nc.tensor.matmul(out, lhsT, rhs, start=start, stop=stop))

    def transp(self, eng_waits, out, in_, ident):
        for w in eng_waits:
            self.pe.wait(w)
        return self.pe.rec(self.nc.tensor.transpose(out, in_, ident))

    def dop(self, name, eng_waits, *args, **kw):
        for w in eng_waits:
            self.dve.wait(w)
        return self.dve.rec(getattr(self.nc.vector, name)(*args, **kw))

    def pop(self, name, eng_waits, *args, **kw):
        for w in eng_waits:
            self.pool.wait(w)
        return self.pool.rec(getattr(self.nc.gpsimd, name)(*args, **kw))

    def aop(self, eng_waits, out, in_, func, **kw):
        for w in eng_waits:
            self.act.wait(w)
        return self.act.rec(self.nc.scalar.activation(out, in_, func, **kw))

    def barrier(self, skip_qg=False):
        toks = [e.tok() for e in self.engs]
        toks.append(("q", self.qs.sem, self.qs.n))
        toks.append(("q", self.qv.sem, self.qv.n))
        if not skip_qg:
            toks.append(("q", self.qg.sem, self.qg.n))
        for e in self.engs:
            for t in toks:
                e.wait(t)
        for t in toks:
            self.qs.wait(t)
            self.qv.wait(t)
            self.qg.wait(t)


def sb(st, nc, name, shape, dtype, side="left"):
    return st.enter_context(nc.sbuf_tensor(name, shape, dtype, side=side))


def sbr(st, nc, name, shape, dtype):
    return sb(st, nc, name, shape, dtype, side="right")


PHASE_MARKS = []


def build_program(dbg=None, upto="all"):
    dbg = dbg or set()
    nc = bass.Bass("TRN2", target_bir_lowering=False)
    PHASE_MARKS.clear()

    def mark(nm):
        PHASE_MARKS.append((nm, sum(1 for _ in nc.all_instructions())))

    x_in = nc.dram_tensor("x", [T, D], F32, kind="ExternalInput").ap()
    wq = nc.dram_tensor("wq", [D, D], F32, kind="ExternalInput").ap()
    wk = nc.dram_tensor("wk", [D, 512], F32, kind="ExternalInput").ap()
    wv = nc.dram_tensor("wv", [D, 512], F32, kind="ExternalInput").ap()
    wo = nc.dram_tensor("wo", [D, D], F32, kind="ExternalInput").ap()
    wrt = nc.dram_tensor("wrt", [D, E], F32, kind="ExternalInput").ap()
    wg_s = nc.dram_tensor("wg_s", [E, D, IFF], F32, kind="ExternalInput").ap()
    wu_s = nc.dram_tensor("wu_s", [E, D, IFF], F32, kind="ExternalInput").ap()
    wd_s = nc.dram_tensor("wd_s", [E, IFF, D], F32, kind="ExternalInput").ap()
    id_bf = nc.dram_tensor("id_bf", [128, 128], BF16, kind="ExternalInput").ap()
    id_f = nc.dram_tensor("id_f", [128, 128], F32, kind="ExternalInput").ap()
    tri16 = nc.dram_tensor("tri16", [128, 128], F32, kind="ExternalInput").ap()
    mkeep = nc.dram_tensor("mkeep", [128, 128], F32, kind="ExternalInput").ap()
    mlow = nc.dram_tensor("mlow", [128, 128], BF16, kind="ExternalInput").ap()
    u1c = nc.dram_tensor("u1c", [128, 128], BF16, kind="ExternalInput").ap()
    u2c = nc.dram_tensor("u2c", [128, 128], BF16, kind="ExternalInput").ap()
    cos_t = nc.dram_tensor("cos_t", [128, NT * 64], F32, kind="ExternalInput").ap()
    sin_t = nc.dram_tensor("sin_t", [128, NT * 64], F32, kind="ExternalInput").ap()
    iota8 = nc.dram_tensor("iota8", [128, E], F32, kind="ExternalInput").ap()
    cvals = nc.dram_tensor("cvals", [128, 4], F32, kind="ExternalInput").ap()

    out = nc.dram_tensor("out", [T, D], F32, kind="ExternalOutput").ap()
    xmid_hbm = nc.dram_tensor("xmid_scr", [TH, D], F32, kind="Internal").ap()
    h2_hbm = nc.dram_tensor("h2_scr", [128, NF * TH], BF16, kind="Internal").ap()
    o_scr = nc.dram_tensor("o_scr", [NG, NT, 128, 512], BF16, kind="Internal").ap()
    wqb = nc.dram_tensor("wqb", [D, D], BF16, kind="Internal").ap()
    wkb = nc.dram_tensor("wkb", [D, 512], BF16, kind="Internal").ap()
    wvb = nc.dram_tensor("wvb", [D, 512], BF16, kind="Internal").ap()
    wob = nc.dram_tensor("wob", [D, D], BF16, kind="Internal").ap()
    wrtb = nc.dram_tensor("wrtb", [D, E], BF16, kind="Internal").ap()
    wgb = nc.dram_tensor("wgb", [E, D, IFF], BF16, kind="Internal").ap()
    wub = nc.dram_tensor("wub", [E, D, IFF], BF16, kind="Internal").ap()
    wdbs = nc.dram_tensor("wdbs", [E, IFF, D], BF16, kind="Internal").ap()

    dbg_out = {}
    if "h_fm" in dbg:
        dbg_out["d_hfm"] = nc.dram_tensor("d_hfm", [128, NF * T], BF16, kind="ExternalOutput").ap()
    if "qk" in dbg:
        for nm in ("d_sk", "d_v", "d_ktl", "d_qh", "d_sq"):
            dbg_out[nm] = nc.dram_tensor(nm, [128, T if nm in ("d_sk", "d_v") else 128],
                                         BF16, kind="ExternalOutput").ap()
        for nm in ("d_lam",):
            dbg_out[nm] = nc.dram_tensor(nm, [128, 128], F32, kind="ExternalOutput").ap()
    if "o" in dbg:
        dbg_out["d_o"] = nc.dram_tensor("d_o", [NG * NT * 128, 512], BF16, kind="ExternalOutput").ap()
    if "xmid" in dbg:
        dbg_out["d_xmid"] = nc.dram_tensor("d_xmid", [T, D], F32, kind="ExternalOutput").ap()
        dbg_out["d_logits"] = nc.dram_tensor("d_logits", [128, NT * E], F32, kind="ExternalOutput").ap()
    if "comb" in dbg:
        dbg_out["d_comb"] = nc.dram_tensor("d_comb", [128, NT * E], F32, kind="ExternalOutput").ap()

    with ExitStack() as st:
        p = Prog(nc, st)
        pf = [st.enter_context(nc.psum_tensor(f"pf{i}", [128, 512], F32)) for i in range(6)]
        pb = [st.enter_context(nc.psum_tensor(f"pb{i}", [128, 1024], BF16)) for i in range(2)]

        stA = ExitStack()
        c_idb = sb(st, nc, "c_idb", [128, 128], BF16)
        c_idf = sb(st, nc, "c_idf", [128, 128], F32)
        c_tri = sb(st, nc, "c_tri", [128, 128], F32)
        c_mkeep = sb(st, nc, "c_mkeep", [128, 128], F32)
        c_mlow = sb(st, nc, "c_mlow", [128, 128], BF16)
        c_u1 = sb(st, nc, "c_u1", [128, 128], BF16)
        c_u2 = sb(st, nc, "c_u2", [128, 128], BF16)
        c_iota = sb(st, nc, "c_iota", [128, E], F32)
        c_vals = sb(st, nc, "c_vals", [128, 4], F32)
        fence_t = sb(st, nc, "fence_t", [128, 128], BF16)
        c_cos = sb(stA, nc, "c_cos", [128, NT * 64], F32)
        c_sin = sb(stA, nc, "c_sin", [128, NT * 64], F32)
        for dst, src in ((c_idb, id_bf), (c_idf, id_f), (c_tri, tri16), (c_mkeep, mkeep),
                         (c_mlow, mlow), (c_u1, u1c), (c_u2, u2c),
                         (c_cos, cos_t), (c_sin, sin_t), (c_iota, iota8),
                         (c_vals, cvals),):
            p.qs.dma(dst[:], src)
        t_const = ("q", p.qs.sem, p.qs.n)
        p.qs.wait(t_const)
        p.qg.dma(wqb, wq)
        p.qg.dma(wkb, wk)
        p.qg.dma(wvb, wv)
        p.qg.dma(wob, wo)
        p.qg.dma(wrtb, wrt)
        p.qg.dma(fence_t[:], wob[D - 128:D, 0:128])
        t_wcq = ("q", p.qg.sem, p.qg.n)
        p.qg.wait(t_wcq)
        for e_ in range(E):
            p.qg.dma(wgb[e_, :, :], wg_s[e_, :, :])
            p.qg.dma(wub[e_, :, :], wu_s[e_, :, :])
            p.qg.dma(wdbs[e_, :, :], wd_s[e_, :, :])
        p.qg.dma(fence_t[:], wdbs[E - 1, IFF - 128:IFF, 0:128])
        t_wcf = ("q", p.qg.sem, p.qg.n)
        nc.const_aps.register(F32, EPS, c_vals[:, 0:1])
        nc.const_aps.register(F32, 0.5, c_vals[:, 1:2])
        nc.const_aps.register(F32, -0.5, c_vals[:, 2:3])
        nc.const_aps.register(F32, ISQD, c_vals[:, 3:4])

        mark("p1")
        h_fm = sb(stA, nc, "h_fm", [128, NF * T], BF16)
        pbq_free = [None, None]
        with ExitStack() as st1:
            xbuf = [sb(st1, nc, f"xbuf{i}", [128, D], F32, side="right") for i in range(4)]
            htok = [sb(st1, nc, f"htok{i}", [128, D], BF16, side="right") for i in range(4)]
            scrD = sb(st1, nc, "scrD", [128, D], F32, side="right")
            stats = [sb(st1, nc, f"stats{i}", [128, 4], F32, side="right") for i in range(4)]
            rstds = [sb(st1, nc, f"rstd{i}", [128, 1], F32, side="right") for i in range(4)]
            xb_free = [[] for _ in range(4)]
            ht_free = [[] for _ in range(4)]
            s1_free = [[] for _ in range(4)]
            for it in range(NT):
                bi = it % 4
                s1 = stats[bi]
                rstd = rstds[bi]
                for tk in xb_free[bi]:
                    p.qs.wait(tk)
                tx = p.qs.dma(xbuf[bi][:], x_in[it * 128:(it + 1) * 128, :])
                p.dop("reduce_sum", [tx] + s1_free[bi], s1[:, 0:1], xbuf[bi][:], axis=AX)
                t0 = p.dve.tok()
                p.dop("tensor_scalar", [t0], s1[:, 2:3], s1[:, 0:1], -1.0 / D, None, ALU.mult)
                tmu = p.dve.tok()
                p.aop([tmu], scrD[:], xbuf[bi][:], AF.Square, bias=s1[:, 2:3], scale=1.0,
                      accum_out=s1[:, 1:2])
                tsq = p.act.tok()
                p.dop("tensor_scalar", [tsq], s1[:, 3:4], s1[:, 1:2], 1.0 / D, None, ALU.mult)
                tvar = p.dve.tok()
                p.aop([tvar], rstd[:], s1[:, 3:4], AF.Ln, bias=EPS, scale=1.0)
                tln = p.act.tok()
                p.aop([tln], rstd[:], rstd[:], AF.Exp, bias=0.0, scale=-0.5)
                trs = p.act.tok()
                p.dop("tensor_scalar", [trs, tmu] + ht_free[bi], htok[bi][:], xbuf[bi][:],
                      s1[:, 2:3], rstd[:], ALU.add, ALU.mult)
                th = p.dve.tok()
                xb_free[bi] = [th, tsq]
                s1_free[bi] = [trs, th]
                for fc in range(NF):
                    bk = fc % 2
                    p.transp([th, t_const, pbq_free[bk]], pb[bk][:, 0:128],
                             htok[bi][:, fc * 128:(fc + 1) * 128], c_idb[:])
                    tt = p.pe.tok()
                    p.dop("tensor_copy", [tt], h_fm[:, fc * T + it * 128: fc * T + (it + 1) * 128],
                          pb[bk][:, 0:128])
                    pbq_free[bk] = p.dve.tok()
                ht_free[bi] = [p.pe.tok()]
            if "d_hfm" in dbg_out:
                p.qs.wait(p.dve.tok())
                p.qs.dma(dbg_out["d_hfm"], h_fm[:])

        p.barrier(skip_qg=True)


        mark("p2")
        kt_scr = nc.dram_tensor("kt_scr", [NG, NT, 128, 128], F32, kind="Internal").ap()
        qt_scr = nc.dram_tensor("qt_scr", [NG, NT, 128, 512], F32, kind="Internal").ap()
        v_scr = nc.dram_tensor("v_scr", [NG, NT, 128, 128], BF16, kind="Internal").ap()
        with ExitStack() as st2:
            wq_g = [sbr(st2, nc, f"wq_g{j}", [128, NF * 512], BF16) for j in range(2)]
            wk_g = [sbr(st2, nc, f"wk_g{j}", [128, NF * 128], BF16) for j in range(2)]
            wv_g = [sbr(st2, nc, f"wv_g{j}", [128, NF * 128], BF16) for j in range(2)]
            qstg = [sbr(st2, nc, f"qstg{j}", [128, 2048], F32) for j in range(2)]
            qtstg = [sbr(st2, nc, f"qtstg{j}", [128, 2048], F32) for j in range(2)]
            kstg = [sbr(st2, nc, f"kstg{j}", [128, 512], F32) for j in range(2)]
            ktstg = [sbr(st2, nc, f"ktstg{j}", [128, 512], F32) for j in range(2)]
            vstg = [sbr(st2, nc, f"vstg{j}", [128, 512], BF16) for j in range(2)]
            gdw = {}
            f_stg = {}
            f_bank = {}

            def load_group(g_):
                gb = g_ % 2
                tkw = gdw.get(f"wgrp{gb}")
                if tkw is not None:
                    p.qs.wait(tkw)
                p.qs.wait(t_wcq)
                nh2 = NF // 2
                for h_ in range(2):
                    p.qs.dma(wq_g[gb][:, h_ * nh2 * 512:(h_ + 1) * nh2 * 512]
                             .rearrange("p (c f) -> p c f", c=nh2),
                             wqb[h_ * nh2 * 128:(h_ + 1) * nh2 * 128,
                                 g_ * 512:(g_ + 1) * 512].rearrange("(c p) f -> p c f", p=128))
                p.qs.dma(wk_g[gb][:].rearrange("p (c f) -> p c f", c=NF),
                         wkb[:, g_ * 128:(g_ + 1) * 128].rearrange("(c p) f -> p c f", p=128))
                p.qs.dma(wv_g[gb][:].rearrange("p (c f) -> p c f", c=NF),
                         wvb[:, g_ * 128:(g_ + 1) * 128].rearrange("(c p) f -> p c f", p=128))
                return ("q", p.qs.sem, p.qs.n)

            tw = load_group(0)
            for g in range(NG):
                gb = g % 2
                if g + 1 < NG:
                    tw_next = load_group(g + 1)
                for s in range(4):
                    sp = s % 2
                    csl = slice(s * 512, (s + 1) * 512)
                    f_evF = {}
                    for hh in range(4):
                        for fc in range(NF):
                            p.mm(([tw] + f_bank.get(hh, [])) if fc == 0 else [],
                                 pf[hh][:, 0:512],
                                 wq_g[gb][:, fc * 512 + hh * 128: fc * 512 + (hh + 1) * 128],
                                 h_fm[:, fc * T + s * 512: fc * T + (s + 1) * 512],
                                 start=(fc == 0), stop=(fc == NF - 1))
                        t_mm = p.pe.tok()
                        p.aop([t_mm] + (f_stg.get(f"qW{sp}", []) if hh == 0 else []),
                              qstg[sp][:, hh * 512:(hh + 1) * 512], pf[hh][:, 0:512], AF.Copy)
                        f_evF[hh] = p.act.tok()
                    for hh in range(4):
                        for vt in range(4):
                            p.transp([f_evF[hh]] if vt == 0 else [],
                                     pf[hh][:, vt * 128:(vt + 1) * 128],
                                     qstg[sp][:, hh * 512 + vt * 128: hh * 512 + (vt + 1) * 128],
                                     c_idf[:])
                        t_tp = p.pe.tok()
                        p.aop([t_tp] + (f_stg.get(f"qT{sp}", []) if hh == 0 else []),
                              qtstg[sp][:, hh * 512:(hh + 1) * 512], pf[hh][:, 0:512], AF.Copy)
                        f_bank[hh] = [p.act.tok()]
                    f_stg[f"qW{sp}"] = [p.pe.tok()]
                    t_qe = p.act.tok()
                    p.qv.wait(t_qe)
                    for hh in range(4):
                        p.qv.dma(qt_scr[g, s * 4:(s + 1) * 4, :, hh * 128:(hh + 1) * 128]
                                 .rearrange("t p c -> p t c"),
                                 qtstg[sp][:, hh * 512:(hh + 1) * 512]
                                 .rearrange("p (t c) -> p t c", t=4))
                    f_stg[f"qT{sp}"] = [("q", p.qv.sem, p.qv.n)]
                    for fc in range(NF):
                        p.mm(([tw] + f_bank.get(4, [])) if fc == 0 else [], pf[4][:, 0:512],
                             wk_g[gb][:, fc * 128:(fc + 1) * 128],
                             h_fm[:, fc * T + s * 512: fc * T + (s + 1) * 512],
                             start=(fc == 0), stop=(fc == NF - 1))
                    t_km = p.pe.tok()
                    p.aop([t_km] + f_stg.get(f"kW{sp}", []), kstg[sp][:], pf[4][:, 0:512], AF.Copy)
                    t_ke = p.act.tok()
                    for vt in range(4):
                        i_ = s * 4 + vt
                        for fc in range(NF):
                            p.mm(([tw] + f_bank.get(5, [])) if (vt == 0 and fc == 0) else [],
                                 pf[5][:, vt * 128:(vt + 1) * 128],
                                 h_fm[:, fc * T + i_ * 128: fc * T + (i_ + 1) * 128],
                                 wv_g[gb][:, fc * 128:(fc + 1) * 128],
                                 start=(fc == 0), stop=(fc == NF - 1))
                    t_vm = p.pe.tok()
                    p.dop("tensor_copy", [t_vm] + f_stg.get(f"v{sp}", []), vstg[sp][:],
                          pf[5][:, 0:512])
                    t_ve = p.dve.tok()
                    f_bank[5] = [t_ve]
                    p.qv.wait(t_ve)
                    p.qv.dma(v_scr[g, s * 4:(s + 1) * 4, :, :].rearrange("t p c -> p t c"),
                             vstg[sp][:].rearrange("p (t c) -> p t c", t=4))
                    f_stg[f"v{sp}"] = [("q", p.qv.sem, p.qv.n)]
                    for vt in range(4):
                        p.transp([t_ke] if vt == 0 else [], pf[4][:, vt * 128:(vt + 1) * 128],
                                 kstg[sp][:, vt * 128:(vt + 1) * 128], c_idf[:])
                    t_ktp = p.pe.tok()
                    f_stg[f"kW{sp}"] = [t_ktp]
                    p.aop([t_ktp] + f_stg.get(f"kT{sp}", []), ktstg[sp][:], pf[4][:, 0:512],
                          AF.Copy)
                    t_kte = p.act.tok()
                    f_bank[4] = [t_kte]
                    p.qv.wait(t_kte)
                    p.qv.dma(kt_scr[g, s * 4:(s + 1) * 4, :, :].rearrange("t p c -> p t c"),
                             ktstg[sp][:].rearrange("p (t c) -> p t c", t=4))
                    f_stg[f"kT{sp}"] = [("q", p.qv.sem, p.qv.n)]
                gdw[f"wgrp{gb}"] = p.pe.tok()
                if g + 1 < NG:
                    tw = tw_next
            t_p2 = ("q", p.qv.sem, p.qv.n)
            p2_bank = dict(f_bank)

        mark("p3")
        with ExitStack() as st3:
            VW = 132
            sk_g = [sbr(st3, nc, f"sk_g{j}", [128, T], BF16) for j in range(NG)]
            v_g = [sbr(st3, nc, f"v_g{j}", [128, NT * VW], BF16) for j in range(NG)]
            s_f = [sbr(st3, nc, f"s_f{j}", [128, 128], F32) for j in range(NG)]
            s_b = [sbr(st3, nc, f"s_b{j}", [128, 128], BF16) for j in range(NG)]
            lamC = [sbr(st3, nc, f"lamC{j}", [128, 1], F32) for j in range(3)]
            ktok = [sbr(st3, nc, f"ktok{j}", [128, 128], F32) for j in range(4)]
            uscr = [sbr(st3, nc, f"uscr{j}", [128, 128], F32) for j in range(2)]
            wscr = [sbr(st3, nc, f"wscr{j}", [128, 128], F32) for j in range(2)]
            lam = [sbr(st3, nc, f"lam{j}", [128, 128], F32) for j in range(2)]
            dts = [sbr(st3, nc, f"dts{j}", [128, 128], F32) for j in range(2)]
            ek = [sbr(st3, nc, f"ek{j}", [128, 128], F32) for j in range(2)]
            ktl_b = [sbr(st3, nc, f"ktl_b{j}", [128, 128], BF16) for j in range(3)]
            ktl_f = [sbr(st3, nc, f"ktl_f{j}", [128, 128], BF16) for j in range(3)]
            kst = [sbr(st3, nc, f"kst{j}", [128, 4], F32) for j in range(2)]
            rt0 = [sbr(st3, nc, f"rt0_{j}", [128, 512], F32) for j in range(2)]
            rt1 = [sbr(st3, nc, f"rt1_{j}", [128, 512], F32) for j in range(2)]
            sq4 = [sbr(st3, nc, f"sq4_{j}", [128, 512], BF16) for j in range(3)]
            sq4t = [sbr(st3, nc, f"sq4t{j}", [128, 512], BF16) for j in range(2)]
            sktok = [sbr(st3, nc, f"sktok{j}", [128, 128], BF16) for j in range(2)]
            qtok4 = [sbr(st3, nc, f"qtok4_{j}", [128, 512], F32) for j in range(4)]
            eq4 = [sbr(st3, nc, f"eq4_{j}", [128, 512], F32) for j in range(2)]
            qh4b = [sbr(st3, nc, f"qh4b{j}", [128, 512], BF16) for j in range(2)]
            qh4f = [sbr(st3, nc, f"qh4f{j}", [128, 512], BF16) for j in range(3)]
            qst = [sbr(st3, nc, f"qst{j}", [128, 16], F32) for j in range(3)]
            pstr = [sbr(st3, nc, f"pstr{j}", [128, T], BF16) for j in range(2)]
            zyp = [sbr(st3, nc, f"zyp{j}", [128, 8], F32) for j in range(2)]
            ptm = [sbr(st3, nc, f"ptm{j}", [128, 128], BF16) for j in range(2)]
            stmp = sbr(st3, nc, "stmp", [128, 128], F32)
            ostg = [sbr(st3, nc, f"ostg{j}", [128, 512], BF16) for j in range(2)]
            o_sb = [sbr(st3, nc, f"o_sb{j}", [128, 128], BF16) for j in range(2)]

            NU = NT * NG
            gd = {f"pf{b}": t for b, t in p2_bank.items()}
            ld_tok = {}
            skg_tok = {}

            def G(key):
                return gd.get(key, [])

            for g in range(NG):
                p.dop("memset", [], s_f[g][:], 0.0)
                p.dop("memset", [], s_b[g][:], 0.0)
                p.dop("memset", [], v_g[g][:], 1.0)
            t_ms = p.dve.tok()
            for g in range(NG):
                gd[f"s_b_w{g}"] = [t_ms]
                gd[f"s_f_w{g}"] = [t_ms]

            def load_u(u):
                i, g = divmod(u, NG)
                m = u % 4
                p.qs.wait(t_p2)
                p.qs.wait(t_ms)
                for tk in G(f"ktokr{m}") + G(f"qtokr{m}"):
                    p.qs.wait(tk)
                p.qs.dma(ktok[m][:], kt_scr[g, i, :, :])
                p.qs.dma(qtok4[m][:], qt_scr[g, i, :, :])
                p.qs.dma(v_g[g][:, i * VW:i * VW + 128], v_scr[g, i, :, :])
                ld_tok[u] = ("q", p.qs.sem, p.qs.n)

            def tail(u):
                i, g = divmod(u, NG)
                po = u % 3
                pi = u % 2
                m = u % 4
                csl = slice(i * 128, (i + 1) * 128)
                c64 = slice(i * 64, (i + 1) * 64)
                t_ld = ld_tok[u]
                ktv = ktok[m][:].rearrange("p (c f) -> p c f", c=2)
                cosb = c_cos[:, c64].unsqueeze(1).broadcast_to([128, 2, 64])
                sinb = c_sin[:, c64].unsqueeze(1).broadcast_to([128, 2, 64])
                p.pop("tensor_tensor", [t_ld] + G(f"rt{pi}"),
                      rt0[pi][:, 0:128].rearrange("p (c f) -> p c f", c=2), ktv, cosb, ALU.mult)
                p.pop("tensor_tensor", [],
                      rt1[pi][:, 0:128].rearrange("p (c f) -> p c f", c=2), ktv, sinb, ALU.mult)
                t_r = p.pool.tok()
                p.pop("tensor_sub", [t_r] + G(f"sktok{pi}"), sktok[pi][:, 0:64],
                      rt0[pi][:, 0:64], rt1[pi][:, 64:128])
                p.pop("tensor_add", [], sktok[pi][:, 64:128], rt0[pi][:, 64:128], rt1[pi][:, 0:64])
                t_skt = p.pool.tok()
                p.transp([t_skt, pbq_free[1]], pb[1][:, 0:128], sktok[pi][:], c_idb[:])
                t_skT = p.pe.tok()
                gd[f"sktok{pi}"] = [t_skT]
                p.dop("tensor_copy", [t_skT], sk_g[g][:, csl], pb[1][:, 0:128])
                t_skg = p.dve.tok()
                pbq_free[1] = t_skg
                skg_tok[u] = t_skg
                p.aop([t_ld] + G(f"uscr{pi}"), uscr[pi][:], ktok[m][:], AF.Exp, scale=-1.0)
                t_u = p.act.tok()
                p.aop([t_u] + G(f"wscr{pi}"), wscr[pi][:], uscr[pi][:], AF.Ln, bias=1.0)
                t_w = p.act.tok()
                gd[f"uscr{pi}"] = [t_w]
                p.dop("reduce_max", [t_ld], kst[pi][:, 0:1], ktok[m][:], axis=AX, negate=True)
                t_nm = p.dve.tok()
                p.aop([t_nm] + G(f"ek{pi}"), ek[pi][:], ktok[m][:], AF.Exp, bias=kst[pi][:, 0:1],
                      accum_out=kst[pi][:, 1:2])
                t_ek = p.act.tok()
                gd[f"ktokr{m}"] = [t_ek, t_nm, t_r]
                p.dop("reciprocal", [t_ek], kst[pi][:, 2:3], kst[pi][:, 1:2])
                t_rk = p.dve.tok()
                p.mm([t_w] + G("pf0"), pf[0][:, 0:128], c_tri[:], wscr[pi][:], start=True, stop=True)
                p.mm([], pf[0][:, 128:256], wscr[pi][:], c_tri[:], start=True, stop=True)
                t_At = p.pe.tok()
                gd[f"wscr{pi}"] = [t_At]
                p.aop([t_At] + G(f"lam{pi}"), lam[pi][:], pf[0][:, 0:128], AF.Exp)
                t_lam = p.act.tok()
                p.aop(G(f"dts{pi}"), dts[pi][:], pf[0][:, 0:128], AF.Exp, scale=-1.0)
                t_dts = p.act.tok()
                p.aop([t_At] + G(f"lamC{po}"), lamC[po][:], pf[0][:, 255:256], AF.Exp)
                t_lamC = p.act.tok()
                gd["pf0"] = [t_lamC, t_dts]
                gd[f"lamC{po}_w"] = [t_lamC]
                p.dop("scalar_tensor_tensor", [t_rk, t_dts] + G(f"ktlb{po}"),
                      ktl_b[po][:], ek[pi][:], kst[pi][:, 2:3], dts[pi][:], ALU.mult, ALU.mult)
                t_ktl = p.dve.tok()
                gd[f"ek{pi}"] = [t_ktl]
                gd[f"dts{pi}"] = [t_ktl]
                gd[f"ktlb{po}_w"] = [t_ktl]
                p.transp([t_ktl, pbq_free[0]], pb[0][:, 0:128], ktl_b[po][:], c_idb[:])
                t_ktlT = p.pe.tok()
                p.dop("tensor_copy", [t_ktlT] + G(f"ktlf{po}"), ktl_f[po][:], pb[0][:, 0:128])
                t_ktlf = p.dve.tok()
                pbq_free[0] = t_ktlf
                gd[f"ktlf{po}_w"] = [t_ktlf]
                q4vv = qtok4[m][:].rearrange("p (r c f) -> p r c f", r=4, c=2)
                cosb4 = c_cos[:, c64].unsqueeze(1).unsqueeze(1).broadcast_to([128, 4, 2, 64])
                sinb4 = c_sin[:, c64].unsqueeze(1).unsqueeze(1).broadcast_to([128, 4, 2, 64])
                p.pop("tensor_tensor", [t_ld, t_skt] + G(f"rt{pi}"),
                      rt0[pi][:].rearrange("p (r c f) -> p r c f", r=4, c=2), q4vv, cosb4, ALU.mult)
                p.pop("tensor_tensor", [],
                      rt1[pi][:].rearrange("p (r c f) -> p r c f", r=4, c=2), q4vv, sinb4, ALU.mult)
                t_qr = p.pool.tok()
                r0v = rt0[pi][:].rearrange("p (r c f) -> p r c f", r=4, c=2)
                r1v = rt1[pi][:].rearrange("p (r c f) -> p r c f", r=4, c=2)
                s4v = sq4t[pi][:].rearrange("p (r c f) -> p r c f", r=4, c=2)
                p.pop("tensor_tensor", [t_qr] + G(f"sq4t{pi}"), s4v[:, :, 0, :], r0v[:, :, 0, :],
                      r1v[:, :, 1, :], ALU.subtract)
                p.pop("tensor_tensor", [], s4v[:, :, 1, :], r0v[:, :, 1, :],
                      r1v[:, :, 0, :], ALU.add)
                t_sq4t = p.pool.tok()
                gd[f"rt{pi}"] = [t_sq4t]
                for hh in range(4):
                    bkb = hh % 2
                    p.transp([t_sq4t, pbq_free[bkb]], pb[bkb][:, 0:128],
                             sq4t[pi][:, hh * 128:(hh + 1) * 128], c_idb[:])
                    t_tt = p.pe.tok()
                    p.dop("tensor_copy", [t_tt] + (G(f"sq4_{po}") if hh == 0 else []),
                          sq4[po][:, hh * 128:(hh + 1) * 128], pb[bkb][:, 0:128])
                    pbq_free[bkb] = p.dve.tok()
                t_sq4v = p.dve.tok()
                gd[f"sq4t{pi}"] = [p.pe.tok()]
                gd[f"sq4_{po}_w"] = [t_sq4v]
                p.dop("reduce_max", [t_ld], qst[po][:, 0:4],
                      qtok4[m][:].rearrange("p (r f) -> p r f", r=4), axis=AX, negate=True)
                t_qnm = p.dve.tok()
                for hh in range(4):
                    p.aop([t_qnm] + (G(f"eq4_{pi}") if hh == 0 else []),
                          eq4[pi][:, hh * 128:(hh + 1) * 128], qtok4[m][:, hh * 128:(hh + 1) * 128],
                          AF.Exp, bias=qst[po][:, hh:hh + 1], accum_out=qst[po][:, 4 + hh:5 + hh])
                t_eq = p.act.tok()
                gd[f"qtokr{m}"] = [t_eq, t_qnm, t_qr]
                p.dop("reciprocal", [t_eq], qst[po][:, 8:12], qst[po][:, 4:8])
                t_rq = p.dve.tok()
                p.dop("tensor_scalar", [t_rq], qst[po][:, 12:16], qst[po][:, 8:12], 0.5, None,
                      ALU.mult)
                t_rq5 = p.dve.tok()
                gd[f"rq5_{po}"] = [t_rq5]
                p.pop("tensor_tensor", [t_eq, t_lam] + G(f"qh4b{pi}"),
                      qh4b[pi][:].rearrange("p (r f) -> p r f", r=4),
                      eq4[pi][:].rearrange("p (r f) -> p r f", r=4),
                      lam[pi][:].unsqueeze(1).broadcast_to([128, 4, 128]), ALU.mult)
                t_qhb = p.pool.tok()
                gd[f"eq4_{pi}"] = [t_qhb]
                gd[f"lam{pi}"] = [t_qhb]
                for hh in range(4):
                    bkb = hh % 2
                    p.transp([t_qhb, pbq_free[bkb]], pb[bkb][:, 0:128],
                             qh4b[pi][:, hh * 128:(hh + 1) * 128], c_idb[:])
                    t_tt = p.pe.tok()
                    p.dop("tensor_copy", [t_tt] + (G(f"qh4f{po}") if hh == 0 else []),
                          qh4f[po][:, hh * 128:(hh + 1) * 128], pb[bkb][:, 0:128])
                    pbq_free[bkb] = p.dve.tok()
                gd[f"qh4f{po}_w"] = [p.dve.tok()]
                gd[f"qh4b{pi}"] = [p.pe.tok()]

            def scores(u, hh, bank):
                i, g = divmod(u, NG)
                po = u % 3
                hp = hh % 2
                nkb = i // 4 + 1
                for kb in range(nkb):
                    j0 = kb * 4
                    j1 = min(i + 1, j0 + 4)
                    for j in range(j0, j1):
                        p.mm(G(f"sq4_{po}_w") + [skg_tok[u]] + G(f"pf{bank}") if j == j0 else [],
                             pf[bank][:, (j - j0) * 128:(j - j0 + 1) * 128],
                             sk_g[g][:, j * 128:(j + 1) * 128],
                             sq4[po][:, hh * 128:(hh + 1) * 128],
                             start=True, stop=(j < i))
                        if j == i:
                            p.mm([t_const], pf[bank][:, (j - j0) * 128:(j - j0 + 1) * 128],
                                 c_u2[:], c_u1[:], start=False, stop=True)
                    t_sc = p.pe.tok()
                    nk = (j1 - j0) * 128
                    p.aop([t_sc] + (G(f"pstr{hp}") if kb == 0 else []),
                          pstr[hp][:, j0 * 128: j0 * 128 + nk], pf[bank][:, 0:nk], AF.Exp,
                          scale=ISQD)
                    t_pex = p.act.tok()
                    gd[f"pf{bank}"] = [t_pex]
                gd[f"pexp{hp}"] = [t_pex]

            def headrest(u, hh):
                i, g = divmod(u, NG)
                po = u % 3
                hp = hh % 2
                t_pex = G(f"pexp{hp}")[0]
                for j in range(i + 1):
                    p.mm([t_pex, ld_tok[u]] + G("pf1") if j == 0 else [], pf[1][:, 0:129],
                         pstr[hp][:, j * 128:(j + 1) * 128],
                         v_g[g][:, j * VW: j * VW + 129],
                         start=(j == 0), stop=(j == i))
                t_ype = p.pe.tok()
                gd[f"pstr{hp}"] = [t_ype]
                p.mm(G(f"ktlf{po}_w") + G(f"qh4f{po}_w") + G("pf4"), pf[4][:, 0:128],
                     ktl_f[po][:], qh4f[po][:, hh * 128:(hh + 1) * 128], start=True, stop=True)
                t_pt = p.pe.tok()
                p.dop("tensor_mul", [t_pt, t_const] + G(f"ptm{hp}"), ptm[hp][:],
                      pf[4][:, 0:128], c_mkeep[:])
                t_ptm = p.dve.tok()
                p.mm([t_ptm], pf[4][:, 128:256], ptm[hp][:],
                     v_g[g][:, i * VW: i * VW + 128], start=True, stop=False)
                p.mm(G(f"s_b_w{g}"), pf[4][:, 128:256], qh4f[po][:, hh * 128:(hh + 1) * 128],
                     s_b[g][:], start=False, stop=True)
                t_gpe = p.pe.tok()
                gd[f"ptm{hp}"] = [t_gpe]
                p.dop("reciprocal", [t_ype] + G(f"rzy{hp}"), zyp[hp][:, 0:1], pf[1][:, 128:129])
                t_rz = p.dve.tok()
                p.dop("tensor_scalar", [t_rz] + G(f"osb{hp}"), o_sb[hp][:], pf[1][:, 0:128],
                      zyp[hp][:, 0:1], 0.5, ALU.mult, ALU.mult)
                t_o1 = p.dve.tok()
                gd["pf1"] = [t_o1]
                gd[f"rzy{hp}"] = [t_o1]
                p.dop("scalar_tensor_tensor", [t_o1, t_gpe] + G(f"rq5_{po}"), o_sb[hp][:],
                      pf[4][:, 128:256], qst[po][:, 12 + hh:13 + hh], o_sb[hp][:],
                      ALU.mult, ALU.add)
                t_o2 = p.dve.tok()
                gd["pf4"] = [t_o2]
                p.transp([t_o2, pbq_free[hp]], pb[hp][:, 0:128], o_sb[hp][:], c_idb[:])
                t_oT = p.pe.tok()
                gd[f"osb{hp}"] = [t_oT]
                p.dop("tensor_copy", [t_oT] + G(f"ostg_dma{u % 2}"),
                      ostg[u % 2][:, hh * 128:(hh + 1) * 128], pb[hp][:, 0:128])
                pbq_free[hp] = p.dve.tok()

            def state(u):
                i, g = divmod(u, NG)
                po = u % 3
                p.mm(G(f"ktlb{po}_w") + [ld_tok[u]] + G("pf3"), pf[3][:, 0:128],
                     ktl_b[po][:], v_g[g][:, i * VW:i * VW + 128], start=True, stop=True)
                t_kv = p.pe.tok()
                gd[f"ktlb{po}"] = [t_kv]
                p.dop("tensor_add", [t_kv] + G(f"s_f_w{g}") + G("stmp"), stmp[:],
                      pf[3][:, 0:128], s_f[g][:])
                t_st = p.dve.tok()
                gd["pf3"] = [t_st]
                p.dop("tensor_scalar", [t_st] + G(f"lamC{po}_w"), s_f[g][:], stmp[:],
                      lamC[po][:], None, ALU.mult)
                t_sf = p.dve.tok()
                gd[f"s_f_w{g}"] = [t_sf]
                gd["stmp"] = [t_sf]
                gd[f"lamC{po}"] = [t_sf]
                p.aop([t_sf, p.pe.tok()], s_b[g][:], s_f[g][:], AF.Copy)
                gd[f"s_b_w{g}"] = [p.act.tok()]
                gd[f"ktlf{po}"] = [p.pe.tok()]

            load_u(0)
            load_u(1)
            load_u(2)
            load_u(3)
            tail(0)
            tail(1)
            for u in range(NU):
                i, g = divmod(u, NG)
                scores(u, 0, 5)
                scores(u, 1, 2)
                if u + 4 < NU:
                    load_u(u + 4)
                headrest(u, 0)
                scores(u, 2, 5)
                headrest(u, 1)
                scores(u, 3, 2)
                headrest(u, 2)
                headrest(u, 3)
                gd[f"sq4_{u % 3}"] = [p.pe.tok()]
                gd[f"qh4f{u % 3}"] = [p.pe.tok()]
                t_oe = p.dve.tok()
                p.qs.wait(t_oe)
                t_od = p.qs.dma(o_scr[g, i, :, :], ostg[u % 2][:])
                gd[f"ostg_dma{u % 2}"] = [t_od]
                if u + 2 < NU:
                    tail(u + 2)
                state(u)

        p.barrier()

        if "d_o" in dbg_out:
            p.qs.dma(dbg_out["d_o"], o_scr.rearrange("g i p c -> (g i p) c"))
            p.barrier()
        f_pf0 = f_pf1 = f_pf2 = f_pf3 = f_pf4 = f_pf5 = None

        mark("p4")
        stA.close()
        stB = ExitStack()
        stB4 = ExitStack()
        x_mid = sb(stB, nc, "x_mid", [128, NTH * D], F32)
        comb_tok = sb(stB, nc, "comb_tok", [128, NT * E], F32)
        h2s = sb(stB, nc, "h2s", [128, NF * TH], BF16)
        h2h = [h2s, h2s]
        wo_full = sbr(stB4, nc, "wo_full", [128, NF * D], BF16)
        wrt_sb = sbr(stB4, nc, "wrt_sb", [128, NF * E], BF16)
        o_t = [sbr(stB4, nc, f"o_t{i}", [128, D], BF16) for i in range(2)]
        xb4 = [sbr(stB4, nc, f"xb4_{i}", [128, D], F32) for i in range(2)]
        h2tok = sbr(stB4, nc, "h2tok", [128, D], BF16)
        logits_sb = sbr(stB4, nc, "logits_sb", [128, NT * E], F32)
        xsum = sbr(stB4, nc, "xsum", [128, NT * 4], F32)
        lstat = sbr(stB4, nc, "lstat", [128, 8], F32)

        p.qs.wait(t_wcq)
        for fc in range(NF):
            p.qs.dma(wrt_sb[:, fc * E:(fc + 1) * E], wrtb[fc * 128:(fc + 1) * 128, :])
        for fc in range(NF):
            p.qs.dma(wo_full[:, fc * D:(fc + 1) * D], wob[fc * 128:(fc + 1) * 128, :])
        t_wo = ("q", p.qs.sem, p.qs.n)

        f_ot = [None, None]
        f_xb = [None, None]
        f_h2t = []
        f_pfy = [None, None]
        for half in range(2):
            hbase = half * NTH
            for tt in range(NTH):
                gt = hbase + tt
                bi = tt % 2
                if f_ot[bi] is not None:
                    p.qg.wait(f_ot[bi])
                for g in range(NG):
                    p.qg.dma(o_t[bi][:, g * 512:(g + 1) * 512], o_scr[g, gt, :, :])
                if f_xb[bi] is not None:
                    p.qg.wait(f_xb[bi])
                tx = p.qg.dma(xb4[bi][:], x_in[gt * 128:(gt + 1) * 128, :])
                t_ld = tx
                for oc in range(4):
                    bk = 4 + oc % 2
                    for h in range(NH):
                        p.mm([t_wo, t_ld, f_pfy[oc % 2]] if h == 0 else [], pf[bk][:, 0:512],
                             o_t[bi][:, h * 128:(h + 1) * 128],
                             wo_full[:, h * D + oc * 512: h * D + (oc + 1) * 512],
                             start=(h == 0), stop=(h == NH - 1))
                    t_mm = p.pe.tok()
                    p.dop("scalar_tensor_tensor", [t_mm, tx],
                          x_mid[:, tt * D + oc * 512: tt * D + (oc + 1) * 512],
                          pf[bk][:, 0:512], 1.0, xb4[bi][:, oc * 512:(oc + 1) * 512],
                          ALU.mult, ALU.add,
                          accum_out=xsum[:, gt * 4 + oc: gt * 4 + oc + 1])
                    f_pfy[oc % 2] = p.dve.tok()
                f_ot[bi] = p.pe.tok()
                t_xm = p.dve.tok()
                f_xb[bi] = t_xm
                if half == 0:
                    p.qg.wait(t_xm)
                    p.qg.dma(xmid_hbm[tt * 128:(tt + 1) * 128, :], x_mid[:, tt * D:(tt + 1) * D])
                p.dop("reduce_sum", [t_xm], lstat[:, 0:1], xsum[:, gt * 4:(gt + 1) * 4], axis=AX)
                t0 = p.dve.tok()
                p.dop("tensor_scalar", [t0], lstat[:, 1:2], lstat[:, 0:1], -1.0 / D, None, ALU.mult)
                t1 = p.dve.tok()
                p.aop([t1] + f_h2t, h2tok[:], x_mid[:, tt * D:(tt + 1) * D], AF.Square,
                      bias=lstat[:, 1:2], accum_out=lstat[:, 2:3])
                t2 = p.act.tok()
                p.dop("tensor_scalar", [t2], lstat[:, 3:4], lstat[:, 2:3], 1.0 / D, None, ALU.mult)
                t3 = p.dve.tok()
                p.aop([t3], lstat[:, 4:5], lstat[:, 3:4], AF.Ln, bias=EPS)
                t4 = p.act.tok()
                p.aop([t4], lstat[:, 5:6], lstat[:, 4:5], AF.Exp, scale=-0.5)
                t5 = p.act.tok()
                p.dop("tensor_scalar", [t5], h2tok[:], x_mid[:, tt * D:(tt + 1) * D],
                      lstat[:, 1:2], lstat[:, 5:6], ALU.add, ALU.mult)
                t_h2 = p.dve.tok()
                for fc in range(NF):
                    bkb = fc % 2
                    p.transp([t_h2, pbq_free[bkb]], pb[bkb][:, 0:128],
                             h2tok[:, fc * 128:(fc + 1) * 128], c_idb[:])
                    t_tt2 = p.pe.tok()
                    p.dop("tensor_copy", [t_tt2],
                          h2h[half][:, fc * TH + tt * 128: fc * TH + (tt + 1) * 128],
                          pb[bkb][:, 0:128])
                    pbq_free[bkb] = p.dve.tok()
                t_h2f = p.dve.tok()
                f_h2t = [p.pe.tok()]
                for fc in range(NF):
                    p.mm([t_h2f, t_const, f_pf3] if fc == 0 else [], pf[3][:, 0:E],
                         h2h[half][:, fc * TH + tt * 128: fc * TH + (tt + 1) * 128],
                         wrt_sb[:, fc * E:(fc + 1) * E], start=(fc == 0), stop=(fc == NF - 1))
                t_lg = p.pe.tok()
                p.dop("tensor_copy", [t_lg], logits_sb[:, gt * E:(gt + 1) * E], pf[3][:, 0:E])
                f_pf3 = p.dve.tok()
            if half == 0:
                th2 = p.dve.tok()
                p.qs.wait(th2)
                th2s = p.qs.dma(h2_hbm, h2s[:])
                p.dve.wait(th2s)

        if "d_xmid" in dbg_out:
            p.barrier()
            p.qs.dma(dbg_out["d_xmid"][0:TH, :], xmid_hbm[:, :])
            for tt in range(NTH):
                p.qs.dma(dbg_out["d_xmid"][TH + tt * 128:TH + (tt + 1) * 128, :],
                         x_mid[:, tt * D:(tt + 1) * D])
            p.qs.dma(dbg_out["d_logits"], logits_sb[:])
        p.barrier()

        if upto != "p4":
            mark("pc")
            with ExitStack() as stPC:
                gate = sbr(stPC, nc, "gate", [128, E], F32)
                m8 = sbr(stPC, nc, "m8", [128, E], F32)
                idxu = sbr(stPC, nc, "idxu", [128, E], mybir.dt.uint32)
                idxf = sbr(stPC, nc, "idxf", [128, E], F32)
                oh1 = sbr(stPC, nc, "oh1", [128, E], F32)
                gst = sbr(stPC, nc, "gst", [128, 4], F32)
                f_cb = []
                for gt in range(NT):
                    lsl = logits_sb[:, gt * E:(gt + 1) * E]
                    p.dop("reduce_max", [], gst[:, 0:1], lsl, axis=AX, negate=True)
                    t0 = p.dve.tok()
                    p.aop([t0] + f_cb, gate[:], lsl, AF.Exp, bias=gst[:, 0:1],
                          accum_out=gst[:, 1:2])
                    t1 = p.act.tok()
                    p.dop("reciprocal", [t1], gst[:, 2:3], gst[:, 1:2])
                    t2 = p.dve.tok()
                    p.dop("tensor_scalar", [t2], gate[:], gate[:], gst[:, 2:3], None, ALU.mult)
                    t3 = p.dve.tok()
                    p.dop("max", [t3], m8[:], gate[:])
                    t4 = p.dve.tok()
                    p.dop("max_index", [t4], idxu[:], m8[:], gate[:])
                    t5 = p.dve.tok()
                    p.dop("tensor_copy", [t5], idxf[:], idxu[:])
                    t6 = p.dve.tok()
                    p.dop("tensor_scalar", [t6, t_const], oh1[:], c_iota[:], idxf[:, 0:1],
                          None, ALU.is_equal)
                    t7 = p.dve.tok()
                    csl8 = comb_tok[:, gt * E:(gt + 1) * E]
                    p.dop("tensor_scalar", [t7], csl8, oh1[:], m8[:, 0:1], None, ALU.mult)
                    t8 = p.dve.tok()
                    p.dop("tensor_scalar", [t8], oh1[:], c_iota[:], idxf[:, 1:2], None, ALU.is_equal)
                    t9 = p.dve.tok()
                    p.dop("scalar_tensor_tensor", [t9], csl8, oh1[:], m8[:, 1:2], csl8,
                          ALU.mult, ALU.add)
                    f_cb = [p.dve.tok()]
                if "d_comb" in dbg_out:
                    p.qs.wait(p.dve.tok())
                    p.qs.dma(dbg_out["d_comb"], comb_tok[:])
            p.barrier()
            stB4.close()

        if upto in ("all", "p5e1", "p5h", "p5bc", "p5gu", "p5none", "p5nostore"):
            mark("p5")
            with ExitStack() as st5:
                gu = sbr(st5, nc, "gu", [128, NIC * TH], BF16)
                wgq = [sbr(st5, nc, f"wgq{i}", [128, NF * 384], BF16) for i in range(2)]
                wuq = [sbr(st5, nc, f"wuq{i}", [128, NF * 384], BF16) for i in range(2)]
                wdb = [sbr(st5, nc, f"wdb{i}", [128, NIC * 512], BF16) for i in range(2)]
                sgu = [sbr(st5, nc, f"sgu{i}", [128, 512], F32) for i in range(2)]
                halves = (1, 0) if upto == "all" else (1,)
                n_exp = 0 if upto in ("p5none",) else (E if upto in ("all", "p5h") else 1)
                QD = [(0, 3), (3, 3), (6, 3), (9, 2)]
                f_wg = [("e", p.pe.sem, 0), ("e", p.pe.sem, 0)]
                f_wd = [("e", p.pe.sem, 0), ("e", p.pe.sem, 0)]

                def load_q(e_, q_):
                    b = q_ % 2
                    ic0, nch = QD[q_]
                    iw = nch * 128
                    p.qs.wait(f_wg[b])
                    p.qs.wait(t_wcf)
                    for fc in range(NF):
                        p.qs.dma(wgq[b][:, fc * iw:(fc + 1) * iw],
                                 wgb[e_, fc * 128:(fc + 1) * 128, ic0 * 128:ic0 * 128 + iw])
                    for fc in range(NF):
                        p.qs.dma(wuq[b][:, fc * iw:(fc + 1) * iw],
                                 wub[e_, fc * 128:(fc + 1) * 128, ic0 * 128:ic0 * 128 + iw])
                    return ("q", p.qs.sem, p.qs.n)

                def load_d(e_, hs_):
                    b = hs_ % 2
                    p.qs.wait(f_wd[b])
                    for ic in range(NIC):
                        p.qs.dma(wdb[b][:, ic * 512:(ic + 1) * 512],
                                 wdbs[e_, ic * 128:(ic + 1) * 128, hs_ * 512:(hs_ + 1) * 512])
                    return ("q", p.qs.sem, p.qs.n)

                for half in halves:
                    hbase = half * NTH
                    if half == 0:
                        p.barrier()
                        for tt in range(NTH):
                            p.qg.dma(x_mid[:, tt * D:(tt + 1) * D], xmid_hbm[tt * 128:(tt + 1) * 128, :])
                        p.qg.dma(h2s[:], h2_hbm)
                        t_xr = ("q", p.qg.sem, p.qg.n)
                        p.dve.wait(t_xr)
                        p.pe.wait(t_xr)
                    t_gu_rd = ("e", p.pe.sem, 0)
                    f_sg = [[], []]
                    t_xev = p.dve.tok()
                    twg = load_q(0, 0) if n_exp else None
                    twd_pend = None
                    for e in range(n_exp):
                        for q in range(4):
                            b = q % 2
                            ic0, nch = QD[q]
                            iw = nch * 128
                            tw_cur = twg
                            if q < 3:
                                twg = load_q(e, q + 1)
                            elif e + 1 < n_exp:
                                twg = load_q(e + 1, 0)
                            for icl in range(nch):
                                ic = ic0 + icl
                                for s2 in range(TH // 512):
                                    gb = 0 if (icl + s2) % 2 == 0 else 2
                                    ub = gb + 1
                                    fg = f_pf0 if gb == 0 else f_pf2
                                    fu = f_pf1 if ub == 1 else f_pf3
                                    for fc in range(NF):
                                        p.mm([tw_cur, fg] if fc == 0 else [], pf[gb][:, 0:512],
                                             wgq[b][:, fc * iw + icl * 128: fc * iw + (icl + 1) * 128],
                                             h2h[half][:, fc * TH + s2 * 512: fc * TH + (s2 + 1) * 512],
                                             start=(fc == 0), stop=(fc == NF - 1))
                                    t_g = p.pe.tok()
                                    for fc in range(NF):
                                        p.mm([fu] if fc == 0 else [], pf[ub][:, 0:512],
                                             wuq[b][:, fc * iw + icl * 128: fc * iw + (icl + 1) * 128],
                                             h2h[half][:, fc * TH + s2 * 512: fc * TH + (s2 + 1) * 512],
                                             start=(fc == 0), stop=(fc == NF - 1))
                                    t_u2 = p.pe.tok()
                                    sp = (icl + s2) % 2
                                    p.aop([t_g] + f_sg[sp], sgu[sp][:], pf[gb][:, 0:512], AF.Silu)
                                    t_sg = p.act.tok()
                                    if gb == 0:
                                        f_pf0 = t_sg
                                    else:
                                        f_pf2 = t_sg
                                    p.dop("tensor_mul", [t_sg, t_u2, t_gu_rd],
                                          gu[:, ic * TH + s2 * 512: ic * TH + (s2 + 1) * 512],
                                          sgu[sp][:], pf[ub][:, 0:512])
                                    t_gf = p.dve.tok()
                                    if ub == 1:
                                        f_pf1 = t_gf
                                    else:
                                        f_pf3 = t_gf
                                    f_sg[sp] = [t_gf]
                            f_wg[b] = p.pe.tok()
                        t_gu = p.dve.tok()
                        twd = twd_pend if twd_pend is not None else load_d(e, 0)
                        for hs in range(4):
                            b = hs % 2
                            twd_cur = twd
                            if hs < 3:
                                twd = load_d(e, hs + 1)
                            for tt in range(NTH):
                                gt = hbase + tt
                                bk = 4 + tt % 2
                                fy = f_pf4 if bk == 4 else f_pf5
                                for ic in range(NIC):
                                    p.mm([twd_cur, t_gu, fy] if ic == 0 else [], pf[bk][:, 0:512],
                                         gu[:, ic * TH + tt * 128: ic * TH + (tt + 1) * 128],
                                         wdb[b][:, ic * 512:(ic + 1) * 512],
                                         start=(ic == 0), stop=(ic == NIC - 1))
                                t_y = p.pe.tok()
                                p.dop("scalar_tensor_tensor", [t_y, t_xev],
                                      x_mid[:, tt * D + hs * 512: tt * D + (hs + 1) * 512],
                                      pf[bk][:, 0:512], comb_tok[:, gt * E + e: gt * E + e + 1],
                                      x_mid[:, tt * D + hs * 512: tt * D + (hs + 1) * 512],
                                      ALU.mult, ALU.add)
                                if bk == 4:
                                    f_pf4 = p.dve.tok()
                                else:
                                    f_pf5 = p.dve.tok()
                            f_wd[b] = p.pe.tok()
                        twd_pend = None
                        t_xev = p.dve.tok()
                        t_gu_rd = p.pe.tok()
                    p.qg.wait(t_xev)
                    for tt in range(NTH):
                        gt = hbase + tt
                        p.qg.dma(out[gt * 128:(gt + 1) * 128, :], x_mid[:, tt * D:(tt + 1) * D])

            p.barrier()


        p.barrier()
        stB.close()
        mark("end")
    return nc



def _consts():
    idf = np.eye(128, dtype=np.float32)
    idb = idf.astype(ml_dtypes.bfloat16)
    s = np.arange(128)[:, None]
    t = np.arange(128)[None, :]
    tri16 = np.where(s <= t, -1.0 / 16.0, 0.0).astype(np.float32)
    mkeep = np.where(s <= t, 1.0, 0.0).astype(np.float32)
    mlow = np.where(t <= s, 1.0, 0.0).astype(ml_dtypes.bfloat16)
    u1 = np.where(s > t, 1.0, 0.0).astype(ml_dtypes.bfloat16)
    u2 = np.where(s <= t, -500.0, 0.0).astype(ml_dtypes.bfloat16)
    inv = (1.0 / ROPE_THETA ** (np.arange(0, HD, 2, dtype=np.float32) / HD)).astype(np.float64)
    ang = np.arange(T, dtype=np.float64)[:, None] * inv[None, :]
    cos = np.cos(ang).astype(np.float32).reshape(NT, 128, 64)
    sin = np.sin(ang).astype(np.float32).reshape(NT, 128, 64)
    cos_t = np.ascontiguousarray(cos.transpose(1, 0, 2).reshape(128, NT * 64))
    sin_t = np.ascontiguousarray(sin.transpose(1, 0, 2).reshape(128, NT * 64))
    iota8 = np.tile(np.arange(E, dtype=np.float32), (128, 1))
    cvals = np.tile(np.array([EPS, 0.5, -0.5, ISQD], dtype=np.float32), (128, 1))
    return dict(id_bf=idb, id_f=idf, tri16=tri16, mkeep=mkeep, mlow=mlow, u1c=u1, u2c=u2,
                cos_t=cos_t, sin_t=sin_t, iota8=iota8, cvals=cvals)


def make_in_maps(L0_attn_hidden_states, L0_attn_Wq, L0_attn_Wk, L0_attn_Wv, L0_attn_Wo,
                 L0_ffn_Wr, L0_ffn_Wg, L0_ffn_Wu, L0_ffn_Wd):
    consts = _consts()
    base = dict(consts)
    base["wq"] = np.ascontiguousarray(L0_attn_Wq, dtype=np.float32)
    base["wk"] = np.ascontiguousarray(L0_attn_Wk, dtype=np.float32)
    base["wv"] = np.ascontiguousarray(L0_attn_Wv, dtype=np.float32)
    base["wo"] = np.ascontiguousarray(L0_attn_Wo, dtype=np.float32)
    base["wrt"] = np.ascontiguousarray(np.asarray(L0_ffn_Wr, dtype=np.float32).T)
    base["wg_s"] = np.ascontiguousarray(np.asarray(L0_ffn_Wg, dtype=np.float32).transpose(0, 2, 1))
    base["wu_s"] = np.ascontiguousarray(np.asarray(L0_ffn_Wu, dtype=np.float32).transpose(0, 2, 1))
    base["wd_s"] = np.ascontiguousarray(np.asarray(L0_ffn_Wd, dtype=np.float32).transpose(0, 2, 1))
    x = np.ascontiguousarray(L0_attn_hidden_states, dtype=np.float32)
    in_maps = []
    for c in range(8):
        m = dict(base)
        m["x"] = np.ascontiguousarray(x[c])
        in_maps.append(m)
    return in_maps


def kernel(**inputs):
    in_maps = make_in_maps(**inputs)
    nc = build_program()
    res = run_bass_kernel_spmd(nc, in_maps, core_ids=list(range(8)))
    return np.stack([np.asarray(r["out"], dtype=np.float32) for r in res.results], 0)


# version 23
# speedup vs baseline: 1.0874x; 1.0100x over previous
import math
from contextlib import ExitStack

import numpy as np
import ml_dtypes

import concourse.bass as bass
import concourse.mybir as mybir
from concourse.bass_utils import run_bass_kernel_spmd

F32 = mybir.dt.float32
BF16 = mybir.dt.bfloat16
AF = mybir.ActivationFunctionType
ALU = mybir.AluOpType
AX = mybir.AxisListType.X

T = 2048
D = 2048
NT = T // 128
NF = D // 128
NH = 16
NKV = 4
HD = 128
NG = 4
E = 8
IFF = 1408
NIC = IFF // 128
CAP = 5120
EPS = 1e-5
ISQD = 1.0 / math.sqrt(HD)
ROPE_THETA = 10000.0
TH = T // 2
NTH = NT // 2


class Q:

    def __init__(self, nc, eng, sem):
        self.eng = eng
        self.sem = sem
        self.n = 0
        self.waited = {}

    def dma(self, out, in_, **kw):
        i = self.eng.dma_start(out=out, in_=in_, **kw)
        self.n += 16
        i.then_inc(self.sem, 16)
        return ("q", self.sem, self.n)

    def wait(self, tok):
        _wait(self.eng, self.waited, tok)


def _wait(eng, waited, tok):
    if tok is None:
        return
    kind, sem, n = tok
    if n <= 0:
        return
    key = id(sem)
    if waited.get(key, 0) >= n:
        return
    eng.wait_ge(sem, n)
    waited[key] = n


class Eng:

    def __init__(self, obj, sem):
        self.obj = obj
        self.sem = sem
        self.n = 0
        self.last = None
        self.last_tok = None
        self.waited = {}

    def rec(self, instr):
        self.last = instr
        self.last_tok = None
        return instr

    def tok(self):
        if self.last is None:
            return ("e", self.sem, 0)
        if self.last_tok is None:
            self.n += 1
            self.last.then_inc(self.sem, 1)
            self.last_tok = self.n
        return ("e", self.sem, self.last_tok)

    def wait(self, tok):
        _wait(self.obj, self.waited, tok)


class Prog:
    def __init__(self, nc, st):
        self.nc = nc
        self.st = st
        sems = {k: st.enter_context(nc.semaphore("s_" + k))
                for k in ("pe", "dve", "act", "pool", "qs", "qg", "qv")}
        self.pe = Eng(nc.tensor, sems["pe"])
        self.dve = Eng(nc.vector, sems["dve"])
        self.act = Eng(nc.scalar, sems["act"])
        self.pool = Eng(nc.gpsimd, sems["pool"])
        self.qs = Q(nc, nc.sync, sems["qs"])
        self.qg = Q(nc, nc.gpsimd, sems["qg"])
        self.qv = Q(nc, nc.scalar, sems["qv"])
        self.engs = [self.pe, self.dve, self.act, self.pool]

    def mm(self, eng_waits, out, lhsT, rhs, start, stop):
        for w in eng_waits:
            self.pe.wait(w)
        return self.pe.rec(self.nc.tensor.matmul(out, lhsT, rhs, start=start, stop=stop))

    def transp(self, eng_waits, out, in_, ident):
        for w in eng_waits:
            self.pe.wait(w)
        return self.pe.rec(self.nc.tensor.transpose(out, in_, ident))

    def dop(self, name, eng_waits, *args, **kw):
        for w in eng_waits:
            self.dve.wait(w)
        return self.dve.rec(getattr(self.nc.vector, name)(*args, **kw))

    def pop(self, name, eng_waits, *args, **kw):
        for w in eng_waits:
            self.pool.wait(w)
        return self.pool.rec(getattr(self.nc.gpsimd, name)(*args, **kw))

    def aop(self, eng_waits, out, in_, func, **kw):
        for w in eng_waits:
            self.act.wait(w)
        return self.act.rec(self.nc.scalar.activation(out, in_, func, **kw))

    def barrier(self, skip_qg=False):
        toks = [e.tok() for e in self.engs]
        toks.append(("q", self.qs.sem, self.qs.n))
        toks.append(("q", self.qv.sem, self.qv.n))
        if not skip_qg:
            toks.append(("q", self.qg.sem, self.qg.n))
        for e in self.engs:
            for t in toks:
                e.wait(t)
        for t in toks:
            self.qs.wait(t)
            self.qv.wait(t)
            self.qg.wait(t)


def sb(st, nc, name, shape, dtype, side="left"):
    return st.enter_context(nc.sbuf_tensor(name, shape, dtype, side=side))


def sbr(st, nc, name, shape, dtype):
    return sb(st, nc, name, shape, dtype, side="right")


PHASE_MARKS = []


def build_program(dbg=None, upto="all"):
    dbg = dbg or set()
    nc = bass.Bass("TRN2", target_bir_lowering=False)
    PHASE_MARKS.clear()

    def mark(nm):
        PHASE_MARKS.append((nm, sum(1 for _ in nc.all_instructions())))

    x_in = nc.dram_tensor("x", [T, D], F32, kind="ExternalInput").ap()
    wq = nc.dram_tensor("wq", [128, NG * NF * 512], F32, kind="ExternalInput").ap()
    wk = nc.dram_tensor("wk", [128, NG * NF * 128], F32, kind="ExternalInput").ap()
    wv = nc.dram_tensor("wv", [128, NG * NF * 128], F32, kind="ExternalInput").ap()
    wo = nc.dram_tensor("wo", [D, D], F32, kind="ExternalInput").ap()
    wrt = nc.dram_tensor("wrt", [D, E], F32, kind="ExternalInput").ap()
    wg_s = nc.dram_tensor("wg_s", [E, D, IFF], F32, kind="ExternalInput").ap()
    wu_s = nc.dram_tensor("wu_s", [E, D, IFF], F32, kind="ExternalInput").ap()
    wd_s = nc.dram_tensor("wd_s", [E, IFF, D], F32, kind="ExternalInput").ap()
    id_bf = nc.dram_tensor("id_bf", [128, 128], BF16, kind="ExternalInput").ap()
    id_f = nc.dram_tensor("id_f", [128, 128], F32, kind="ExternalInput").ap()
    tri16 = nc.dram_tensor("tri16", [128, 128], F32, kind="ExternalInput").ap()
    mkeep = nc.dram_tensor("mkeep", [128, 128], F32, kind="ExternalInput").ap()
    mlow = nc.dram_tensor("mlow", [128, 128], BF16, kind="ExternalInput").ap()
    u1c = nc.dram_tensor("u1c", [128, 128], BF16, kind="ExternalInput").ap()
    u2c = nc.dram_tensor("u2c", [128, 128], BF16, kind="ExternalInput").ap()
    cos_t = nc.dram_tensor("cos_t", [128, NT * 64], F32, kind="ExternalInput").ap()
    sin_t = nc.dram_tensor("sin_t", [128, NT * 64], F32, kind="ExternalInput").ap()
    iota8 = nc.dram_tensor("iota8", [128, E], F32, kind="ExternalInput").ap()
    cvals = nc.dram_tensor("cvals", [128, 4], F32, kind="ExternalInput").ap()

    out = nc.dram_tensor("out", [T, D], F32, kind="ExternalOutput").ap()
    xmid_hbm = nc.dram_tensor("xmid_scr", [TH, D], F32, kind="Internal").ap()
    h2_hbm = nc.dram_tensor("h2_scr", [128, NF * TH], BF16, kind="Internal").ap()
    o_scr = nc.dram_tensor("o_scr", [NG, NT, 128, 512], BF16, kind="Internal").ap()
    wqb = nc.dram_tensor("wqb", [128, NG * NF * 512], BF16, kind="Internal").ap()
    wkb = nc.dram_tensor("wkb", [128, NG * NF * 128], BF16, kind="Internal").ap()
    wvb = nc.dram_tensor("wvb", [128, NG * NF * 128], BF16, kind="Internal").ap()
    wob = nc.dram_tensor("wob", [D, D], BF16, kind="Internal").ap()
    wrtb = nc.dram_tensor("wrtb", [D, E], BF16, kind="Internal").ap()
    wgb = nc.dram_tensor("wgb", [E, D, IFF], BF16, kind="Internal").ap()
    wub = nc.dram_tensor("wub", [E, D, IFF], BF16, kind="Internal").ap()
    wdbs = nc.dram_tensor("wdbs", [E, IFF, D], BF16, kind="Internal").ap()

    dbg_out = {}
    if "h_fm" in dbg:
        dbg_out["d_hfm"] = nc.dram_tensor("d_hfm", [128, NF * T], BF16, kind="ExternalOutput").ap()
    if "qk" in dbg:
        for nm in ("d_sk", "d_v", "d_ktl", "d_qh", "d_sq"):
            dbg_out[nm] = nc.dram_tensor(nm, [128, T if nm in ("d_sk", "d_v") else 128],
                                         BF16, kind="ExternalOutput").ap()
        for nm in ("d_lam",):
            dbg_out[nm] = nc.dram_tensor(nm, [128, 128], F32, kind="ExternalOutput").ap()
    if "o" in dbg:
        dbg_out["d_o"] = nc.dram_tensor("d_o", [NG * NT * 128, 512], BF16, kind="ExternalOutput").ap()
    if "xmid" in dbg:
        dbg_out["d_xmid"] = nc.dram_tensor("d_xmid", [T, D], F32, kind="ExternalOutput").ap()
        dbg_out["d_logits"] = nc.dram_tensor("d_logits", [128, NT * E], F32, kind="ExternalOutput").ap()
    if "comb" in dbg:
        dbg_out["d_comb"] = nc.dram_tensor("d_comb", [128, NT * E], F32, kind="ExternalOutput").ap()

    with ExitStack() as st:
        p = Prog(nc, st)
        pf = [st.enter_context(nc.psum_tensor(f"pf{i}", [128, 512], F32)) for i in range(6)]
        pb = [st.enter_context(nc.psum_tensor(f"pb{i}", [128, 1024], BF16)) for i in range(2)]

        stA = ExitStack()
        c_idb = sb(st, nc, "c_idb", [128, 128], BF16)
        c_idf = sb(st, nc, "c_idf", [128, 128], F32)
        c_tri = sb(st, nc, "c_tri", [128, 128], F32)
        c_mkeep = sb(st, nc, "c_mkeep", [128, 128], F32)
        c_mlow = sb(st, nc, "c_mlow", [128, 128], BF16)
        c_u1 = sb(st, nc, "c_u1", [128, 128], BF16)
        c_u2 = sb(st, nc, "c_u2", [128, 128], BF16)
        c_iota = sb(st, nc, "c_iota", [128, E], F32)
        c_vals = sb(st, nc, "c_vals", [128, 4], F32)
        fence_t = sb(st, nc, "fence_t", [128, 128], BF16)
        c_cos = sb(stA, nc, "c_cos", [128, NT * 64], F32)
        c_sin = sb(stA, nc, "c_sin", [128, NT * 64], F32)
        for dst, src in ((c_idb, id_bf), (c_idf, id_f), (c_tri, tri16), (c_mkeep, mkeep),
                         (c_mlow, mlow), (c_u1, u1c), (c_u2, u2c),
                         (c_cos, cos_t), (c_sin, sin_t), (c_iota, iota8),
                         (c_vals, cvals),):
            p.qs.dma(dst[:], src)
        t_const = ("q", p.qs.sem, p.qs.n)
        p.qs.wait(t_const)
        p.qg.dma(wqb, wq)
        p.qg.dma(wkb, wk)
        p.qg.dma(wvb, wv)
        p.qg.dma(wob, wo)
        p.qg.dma(wrtb, wrt)
        p.qg.dma(fence_t[:], wob[D - 128:D, 0:128])
        t_wcq = ("q", p.qg.sem, p.qg.n)
        p.qg.wait(t_wcq)
        for e_ in range(E):
            p.qg.dma(wgb[e_, :, :], wg_s[e_, :, :])
            p.qg.dma(wub[e_, :, :], wu_s[e_, :, :])
            p.qg.dma(wdbs[e_, :, :], wd_s[e_, :, :])
        p.qg.dma(fence_t[:], wdbs[E - 1, IFF - 128:IFF, 0:128])
        t_wcf = ("q", p.qg.sem, p.qg.n)
        nc.const_aps.register(F32, EPS, c_vals[:, 0:1])
        nc.const_aps.register(F32, 0.5, c_vals[:, 1:2])
        nc.const_aps.register(F32, -0.5, c_vals[:, 2:3])
        nc.const_aps.register(F32, ISQD, c_vals[:, 3:4])

        mark("p1")
        h_fm = sb(stA, nc, "h_fm", [128, NF * T], BF16)
        pbq_free = [None, None]
        with ExitStack() as st1:
            xbuf = [sb(st1, nc, f"xbuf{i}", [128, D], F32, side="right") for i in range(4)]
            htok = [sb(st1, nc, f"htok{i}", [128, D], BF16, side="right") for i in range(4)]
            scrD = sb(st1, nc, "scrD", [128, D], F32, side="right")
            stats = [sb(st1, nc, f"stats{i}", [128, 4], F32, side="right") for i in range(4)]
            rstds = [sb(st1, nc, f"rstd{i}", [128, 1], F32, side="right") for i in range(4)]
            xb_free = [[] for _ in range(4)]
            ht_free = [[] for _ in range(4)]
            s1_free = [[] for _ in range(4)]
            for it in range(NT):
                bi = it % 4
                s1 = stats[bi]
                rstd = rstds[bi]
                for tk in xb_free[bi]:
                    p.qs.wait(tk)
                tx = p.qs.dma(xbuf[bi][:], x_in[it * 128:(it + 1) * 128, :])
                p.dop("reduce_sum", [tx] + s1_free[bi], s1[:, 0:1], xbuf[bi][:], axis=AX)
                t0 = p.dve.tok()
                p.dop("tensor_scalar", [t0], s1[:, 2:3], s1[:, 0:1], -1.0 / D, None, ALU.mult)
                tmu = p.dve.tok()
                p.aop([tmu], scrD[:], xbuf[bi][:], AF.Square, bias=s1[:, 2:3], scale=1.0,
                      accum_out=s1[:, 1:2])
                tsq = p.act.tok()
                p.dop("tensor_scalar", [tsq], s1[:, 3:4], s1[:, 1:2], 1.0 / D, None, ALU.mult)
                tvar = p.dve.tok()
                p.aop([tvar], rstd[:], s1[:, 3:4], AF.Ln, bias=EPS, scale=1.0)
                tln = p.act.tok()
                p.aop([tln], rstd[:], rstd[:], AF.Exp, bias=0.0, scale=-0.5)
                trs = p.act.tok()
                p.dop("tensor_scalar", [trs, tmu] + ht_free[bi], htok[bi][:], xbuf[bi][:],
                      s1[:, 2:3], rstd[:], ALU.add, ALU.mult)
                th = p.dve.tok()
                xb_free[bi] = [th, tsq]
                s1_free[bi] = [trs, th]
                for fc in range(NF):
                    bk = fc % 2
                    p.transp([th, t_const, pbq_free[bk]], pb[bk][:, 0:128],
                             htok[bi][:, fc * 128:(fc + 1) * 128], c_idb[:])
                    tt = p.pe.tok()
                    p.dop("tensor_copy", [tt], h_fm[:, fc * T + it * 128: fc * T + (it + 1) * 128],
                          pb[bk][:, 0:128])
                    pbq_free[bk] = p.dve.tok()
                ht_free[bi] = [p.pe.tok()]
            if "d_hfm" in dbg_out:
                p.qs.wait(p.dve.tok())
                p.qs.dma(dbg_out["d_hfm"], h_fm[:])

        p.barrier(skip_qg=True)


        mark("p2")
        kt_scr = nc.dram_tensor("kt_scr", [NG, NT, 128, 128], F32, kind="Internal").ap()
        qt_scr = nc.dram_tensor("qt_scr", [NG, NT, 128, 512], F32, kind="Internal").ap()
        v_scr = nc.dram_tensor("v_scr", [NG, NT, 128, 128], BF16, kind="Internal").ap()
        with ExitStack() as st2:
            wq_g = [sbr(st2, nc, f"wq_g{j}", [128, NF * 512], BF16) for j in range(2)]
            wk_g = [sbr(st2, nc, f"wk_g{j}", [128, NF * 128], BF16) for j in range(2)]
            wv_g = [sbr(st2, nc, f"wv_g{j}", [128, NF * 128], BF16) for j in range(2)]
            qstg = [sbr(st2, nc, f"qstg{j}", [128, 2048], F32) for j in range(2)]
            qtstg = [sbr(st2, nc, f"qtstg{j}", [128, 2048], F32) for j in range(2)]
            kstg = [sbr(st2, nc, f"kstg{j}", [128, 512], F32) for j in range(2)]
            ktstg = [sbr(st2, nc, f"ktstg{j}", [128, 512], F32) for j in range(2)]
            vstg = [sbr(st2, nc, f"vstg{j}", [128, 512], BF16) for j in range(2)]
            gdw = {}
            f_stg = {}
            f_bank = {}

            def load_group(g_):
                gb = g_ % 2
                tkw = gdw.get(f"wgrp{gb}")
                if tkw is not None:
                    p.qs.wait(tkw)
                p.qs.wait(t_wcq)
                p.qs.dma(wq_g[gb][:], wqb[:, g_ * NF * 512:(g_ + 1) * NF * 512])
                p.qs.dma(wk_g[gb][:], wkb[:, g_ * NF * 128:(g_ + 1) * NF * 128])
                p.qs.dma(wv_g[gb][:], wvb[:, g_ * NF * 128:(g_ + 1) * NF * 128])
                return ("q", p.qs.sem, p.qs.n)

            tw = load_group(0)
            for g in range(NG):
                gb = g % 2
                if g + 1 < NG:
                    tw_next = load_group(g + 1)
                for s in range(4):
                    sp = s % 2
                    csl = slice(s * 512, (s + 1) * 512)
                    f_evF = {}
                    for hh in range(4):
                        for fc in range(NF):
                            p.mm(([tw] + f_bank.get(hh, [])) if fc == 0 else [],
                                 pf[hh][:, 0:512],
                                 wq_g[gb][:, fc * 512 + hh * 128: fc * 512 + (hh + 1) * 128],
                                 h_fm[:, fc * T + s * 512: fc * T + (s + 1) * 512],
                                 start=(fc == 0), stop=(fc == NF - 1))
                        t_mm = p.pe.tok()
                        p.aop([t_mm] + (f_stg.get(f"qW{sp}", []) if hh == 0 else []),
                              qstg[sp][:, hh * 512:(hh + 1) * 512], pf[hh][:, 0:512], AF.Copy)
                        f_evF[hh] = p.act.tok()
                    for hh in range(4):
                        for vt in range(4):
                            p.transp([f_evF[hh]] if vt == 0 else [],
                                     pf[hh][:, vt * 128:(vt + 1) * 128],
                                     qstg[sp][:, hh * 512 + vt * 128: hh * 512 + (vt + 1) * 128],
                                     c_idf[:])
                        t_tp = p.pe.tok()
                        p.aop([t_tp] + (f_stg.get(f"qT{sp}", []) if hh == 0 else []),
                              qtstg[sp][:, hh * 512:(hh + 1) * 512], pf[hh][:, 0:512], AF.Copy)
                        f_bank[hh] = [p.act.tok()]
                    f_stg[f"qW{sp}"] = [p.pe.tok()]
                    t_qe = p.act.tok()
                    p.qv.wait(t_qe)
                    for hh in range(4):
                        p.qv.dma(qt_scr[g, s * 4:(s + 1) * 4, :, hh * 128:(hh + 1) * 128]
                                 .rearrange("t p c -> p t c"),
                                 qtstg[sp][:, hh * 512:(hh + 1) * 512]
                                 .rearrange("p (t c) -> p t c", t=4))
                    f_stg[f"qT{sp}"] = [("q", p.qv.sem, p.qv.n)]
                    for fc in range(NF):
                        p.mm(([tw] + f_bank.get(4, [])) if fc == 0 else [], pf[4][:, 0:512],
                             wk_g[gb][:, fc * 128:(fc + 1) * 128],
                             h_fm[:, fc * T + s * 512: fc * T + (s + 1) * 512],
                             start=(fc == 0), stop=(fc == NF - 1))
                    t_km = p.pe.tok()
                    p.aop([t_km] + f_stg.get(f"kW{sp}", []), kstg[sp][:], pf[4][:, 0:512], AF.Copy)
                    t_ke = p.act.tok()
                    for vt in range(4):
                        i_ = s * 4 + vt
                        for fc in range(NF):
                            p.mm(([tw] + f_bank.get(5, [])) if (vt == 0 and fc == 0) else [],
                                 pf[5][:, vt * 128:(vt + 1) * 128],
                                 h_fm[:, fc * T + i_ * 128: fc * T + (i_ + 1) * 128],
                                 wv_g[gb][:, fc * 128:(fc + 1) * 128],
                                 start=(fc == 0), stop=(fc == NF - 1))
                    t_vm = p.pe.tok()
                    p.dop("tensor_copy", [t_vm] + f_stg.get(f"v{sp}", []), vstg[sp][:],
                          pf[5][:, 0:512])
                    t_ve = p.dve.tok()
                    f_bank[5] = [t_ve]
                    p.qv.wait(t_ve)
                    p.qv.dma(v_scr[g, s * 4:(s + 1) * 4, :, :].rearrange("t p c -> p t c"),
                             vstg[sp][:].rearrange("p (t c) -> p t c", t=4))
                    f_stg[f"v{sp}"] = [("q", p.qv.sem, p.qv.n)]
                    for vt in range(4):
                        p.transp([t_ke] if vt == 0 else [], pf[4][:, vt * 128:(vt + 1) * 128],
                                 kstg[sp][:, vt * 128:(vt + 1) * 128], c_idf[:])
                    t_ktp = p.pe.tok()
                    f_stg[f"kW{sp}"] = [t_ktp]
                    p.aop([t_ktp] + f_stg.get(f"kT{sp}", []), ktstg[sp][:], pf[4][:, 0:512],
                          AF.Copy)
                    t_kte = p.act.tok()
                    f_bank[4] = [t_kte]
                    p.qv.wait(t_kte)
                    p.qv.dma(kt_scr[g, s * 4:(s + 1) * 4, :, :].rearrange("t p c -> p t c"),
                             ktstg[sp][:].rearrange("p (t c) -> p t c", t=4))
                    f_stg[f"kT{sp}"] = [("q", p.qv.sem, p.qv.n)]
                gdw[f"wgrp{gb}"] = p.pe.tok()
                if g + 1 < NG:
                    tw = tw_next
            t_p2 = ("q", p.qv.sem, p.qv.n)
            p2_bank = dict(f_bank)

        mark("p3")
        with ExitStack() as st3:
            VW = 132
            sk_g = [sbr(st3, nc, f"sk_g{j}", [128, T], BF16) for j in range(NG)]
            v_g = [sbr(st3, nc, f"v_g{j}", [128, NT * VW], BF16) for j in range(NG)]
            s_f = [sbr(st3, nc, f"s_f{j}", [128, 128], F32) for j in range(NG)]
            s_b = [sbr(st3, nc, f"s_b{j}", [128, 128], BF16) for j in range(NG)]
            lamC = [sbr(st3, nc, f"lamC{j}", [128, 1], F32) for j in range(3)]
            ktok = [sbr(st3, nc, f"ktok{j}", [128, 128], F32) for j in range(4)]
            uscr = [sbr(st3, nc, f"uscr{j}", [128, 128], F32) for j in range(2)]
            wscr = [sbr(st3, nc, f"wscr{j}", [128, 128], F32) for j in range(2)]
            lam = [sbr(st3, nc, f"lam{j}", [128, 128], F32) for j in range(2)]
            dts = [sbr(st3, nc, f"dts{j}", [128, 128], F32) for j in range(2)]
            ek = [sbr(st3, nc, f"ek{j}", [128, 128], F32) for j in range(2)]
            ktl_b = [sbr(st3, nc, f"ktl_b{j}", [128, 128], BF16) for j in range(3)]
            ktl_f = [sbr(st3, nc, f"ktl_f{j}", [128, 128], BF16) for j in range(3)]
            kst = [sbr(st3, nc, f"kst{j}", [128, 4], F32) for j in range(2)]
            rt0 = [sbr(st3, nc, f"rt0_{j}", [128, 512], F32) for j in range(2)]
            rt1 = [sbr(st3, nc, f"rt1_{j}", [128, 512], F32) for j in range(2)]
            sq4 = [sbr(st3, nc, f"sq4_{j}", [128, 512], BF16) for j in range(3)]
            sq4t = [sbr(st3, nc, f"sq4t{j}", [128, 512], BF16) for j in range(2)]
            sktok = [sbr(st3, nc, f"sktok{j}", [128, 128], BF16) for j in range(2)]
            qtok4 = [sbr(st3, nc, f"qtok4_{j}", [128, 512], F32) for j in range(4)]
            eq4 = [sbr(st3, nc, f"eq4_{j}", [128, 512], F32) for j in range(2)]
            qh4b = [sbr(st3, nc, f"qh4b{j}", [128, 512], BF16) for j in range(2)]
            qh4f = [sbr(st3, nc, f"qh4f{j}", [128, 512], BF16) for j in range(3)]
            qst = [sbr(st3, nc, f"qst{j}", [128, 16], F32) for j in range(3)]
            pstr = [sbr(st3, nc, f"pstr{j}", [128, T], BF16) for j in range(2)]
            zyp = [sbr(st3, nc, f"zyp{j}", [128, 8], F32) for j in range(2)]
            ptm = [sbr(st3, nc, f"ptm{j}", [128, 128], BF16) for j in range(2)]
            stmp = sbr(st3, nc, "stmp", [128, 128], F32)
            ostg = [sbr(st3, nc, f"ostg{j}", [128, 512], BF16) for j in range(2)]
            o_sb = [sbr(st3, nc, f"o_sb{j}", [128, 128], BF16) for j in range(2)]

            NU = NT * NG
            gd = {f"pf{b}": t for b, t in p2_bank.items()}
            ld_tok = {}
            skg_tok = {}

            def G(key):
                return gd.get(key, [])

            for g in range(NG):
                p.dop("memset", [], s_f[g][:], 0.0)
                p.dop("memset", [], s_b[g][:], 0.0)
                p.dop("memset", [], v_g[g][:], 1.0)
            t_ms = p.dve.tok()
            for g in range(NG):
                gd[f"s_b_w{g}"] = [t_ms]
                gd[f"s_f_w{g}"] = [t_ms]

            def load_u(u):
                i, g = divmod(u, NG)
                m = u % 4
                p.qs.wait(t_p2)
                p.qs.wait(t_ms)
                for tk in G(f"ktokr{m}") + G(f"qtokr{m}"):
                    p.qs.wait(tk)
                p.qs.dma(ktok[m][:], kt_scr[g, i, :, :])
                p.qs.dma(qtok4[m][:], qt_scr[g, i, :, :])
                p.qs.dma(v_g[g][:, i * VW:i * VW + 128], v_scr[g, i, :, :])
                ld_tok[u] = ("q", p.qs.sem, p.qs.n)

            def tail(u):
                i, g = divmod(u, NG)
                po = u % 3
                pi = u % 2
                m = u % 4
                csl = slice(i * 128, (i + 1) * 128)
                c64 = slice(i * 64, (i + 1) * 64)
                t_ld = ld_tok[u]
                ktv = ktok[m][:].rearrange("p (c f) -> p c f", c=2)
                cosb = c_cos[:, c64].unsqueeze(1).broadcast_to([128, 2, 64])
                sinb = c_sin[:, c64].unsqueeze(1).broadcast_to([128, 2, 64])
                p.pop("tensor_tensor", [t_ld] + G(f"rt{pi}"),
                      rt0[pi][:, 0:128].rearrange("p (c f) -> p c f", c=2), ktv, cosb, ALU.mult)
                p.pop("tensor_tensor", [],
                      rt1[pi][:, 0:128].rearrange("p (c f) -> p c f", c=2), ktv, sinb, ALU.mult)
                t_r = p.pool.tok()
                p.pop("tensor_sub", [t_r] + G(f"sktok{pi}"), sktok[pi][:, 0:64],
                      rt0[pi][:, 0:64], rt1[pi][:, 64:128])
                p.pop("tensor_add", [], sktok[pi][:, 64:128], rt0[pi][:, 64:128], rt1[pi][:, 0:64])
                t_skt = p.pool.tok()
                p.transp([t_skt, pbq_free[1]], pb[1][:, 0:128], sktok[pi][:], c_idb[:])
                t_skT = p.pe.tok()
                gd[f"sktok{pi}"] = [t_skT]
                p.dop("tensor_copy", [t_skT], sk_g[g][:, csl], pb[1][:, 0:128])
                t_skg = p.dve.tok()
                pbq_free[1] = t_skg
                skg_tok[u] = t_skg
                p.aop([t_ld] + G(f"uscr{pi}"), uscr[pi][:], ktok[m][:], AF.Exp, scale=-1.0)
                t_u = p.act.tok()
                p.aop([t_u] + G(f"wscr{pi}"), wscr[pi][:], uscr[pi][:], AF.Ln, bias=1.0)
                t_w = p.act.tok()
                gd[f"uscr{pi}"] = [t_w]
                p.dop("reduce_max", [t_ld], kst[pi][:, 0:1], ktok[m][:], axis=AX, negate=True)
                t_nm = p.dve.tok()
                p.aop([t_nm] + G(f"ek{pi}"), ek[pi][:], ktok[m][:], AF.Exp, bias=kst[pi][:, 0:1],
                      accum_out=kst[pi][:, 1:2])
                t_ek = p.act.tok()
                gd[f"ktokr{m}"] = [t_ek, t_nm, t_r]
                p.dop("reciprocal", [t_ek], kst[pi][:, 2:3], kst[pi][:, 1:2])
                t_rk = p.dve.tok()
                p.mm([t_w] + G("pf0"), pf[0][:, 0:128], c_tri[:], wscr[pi][:], start=True, stop=True)
                p.mm([], pf[0][:, 128:256], wscr[pi][:], c_tri[:], start=True, stop=True)
                t_At = p.pe.tok()
                gd[f"wscr{pi}"] = [t_At]
                p.aop([t_At] + G(f"lam{pi}"), lam[pi][:], pf[0][:, 0:128], AF.Exp)
                t_lam = p.act.tok()
                p.aop(G(f"dts{pi}"), dts[pi][:], pf[0][:, 0:128], AF.Exp, scale=-1.0)
                t_dts = p.act.tok()
                p.aop([t_At] + G(f"lamC{po}"), lamC[po][:], pf[0][:, 255:256], AF.Exp)
                t_lamC = p.act.tok()
                gd["pf0"] = [t_lamC, t_dts]
                gd[f"lamC{po}_w"] = [t_lamC]
                p.dop("scalar_tensor_tensor", [t_rk, t_dts] + G(f"ktlb{po}"),
                      ktl_b[po][:], ek[pi][:], kst[pi][:, 2:3], dts[pi][:], ALU.mult, ALU.mult)
                t_ktl = p.dve.tok()
                gd[f"ek{pi}"] = [t_ktl]
                gd[f"dts{pi}"] = [t_ktl]
                gd[f"ktlb{po}_w"] = [t_ktl]
                p.transp([t_ktl, pbq_free[0]], pb[0][:, 0:128], ktl_b[po][:], c_idb[:])
                t_ktlT = p.pe.tok()
                p.dop("tensor_copy", [t_ktlT] + G(f"ktlf{po}"), ktl_f[po][:], pb[0][:, 0:128])
                t_ktlf = p.dve.tok()
                pbq_free[0] = t_ktlf
                gd[f"ktlf{po}_w"] = [t_ktlf]
                q4vv = qtok4[m][:].rearrange("p (r c f) -> p r c f", r=4, c=2)
                cosb4 = c_cos[:, c64].unsqueeze(1).unsqueeze(1).broadcast_to([128, 4, 2, 64])
                sinb4 = c_sin[:, c64].unsqueeze(1).unsqueeze(1).broadcast_to([128, 4, 2, 64])
                p.pop("tensor_tensor", [t_ld, t_skt] + G(f"rt{pi}"),
                      rt0[pi][:].rearrange("p (r c f) -> p r c f", r=4, c=2), q4vv, cosb4, ALU.mult)
                p.pop("tensor_tensor", [],
                      rt1[pi][:].rearrange("p (r c f) -> p r c f", r=4, c=2), q4vv, sinb4, ALU.mult)
                t_qr = p.pool.tok()
                r0v = rt0[pi][:].rearrange("p (r c f) -> p r c f", r=4, c=2)
                r1v = rt1[pi][:].rearrange("p (r c f) -> p r c f", r=4, c=2)
                s4v = sq4t[pi][:].rearrange("p (r c f) -> p r c f", r=4, c=2)
                p.pop("tensor_tensor", [t_qr] + G(f"sq4t{pi}"), s4v[:, :, 0, :], r0v[:, :, 0, :],
                      r1v[:, :, 1, :], ALU.subtract)
                p.pop("tensor_tensor", [], s4v[:, :, 1, :], r0v[:, :, 1, :],
                      r1v[:, :, 0, :], ALU.add)
                t_sq4t = p.pool.tok()
                gd[f"rt{pi}"] = [t_sq4t]
                for hh in range(4):
                    bkb = hh % 2
                    p.transp([t_sq4t, pbq_free[bkb]], pb[bkb][:, 0:128],
                             sq4t[pi][:, hh * 128:(hh + 1) * 128], c_idb[:])
                    t_tt = p.pe.tok()
                    p.dop("tensor_copy", [t_tt] + (G(f"sq4_{po}") if hh == 0 else []),
                          sq4[po][:, hh * 128:(hh + 1) * 128], pb[bkb][:, 0:128])
                    pbq_free[bkb] = p.dve.tok()
                t_sq4v = p.dve.tok()
                gd[f"sq4t{pi}"] = [p.pe.tok()]
                gd[f"sq4_{po}_w"] = [t_sq4v]
                p.dop("reduce_max", [t_ld], qst[po][:, 0:4],
                      qtok4[m][:].rearrange("p (r f) -> p r f", r=4), axis=AX, negate=True)
                t_qnm = p.dve.tok()
                for hh in range(4):
                    p.aop([t_qnm] + (G(f"eq4_{pi}") if hh == 0 else []),
                          eq4[pi][:, hh * 128:(hh + 1) * 128], qtok4[m][:, hh * 128:(hh + 1) * 128],
                          AF.Exp, bias=qst[po][:, hh:hh + 1], accum_out=qst[po][:, 4 + hh:5 + hh])
                t_eq = p.act.tok()
                gd[f"qtokr{m}"] = [t_eq, t_qnm, t_qr]
                p.dop("reciprocal", [t_eq], qst[po][:, 8:12], qst[po][:, 4:8])
                t_rq = p.dve.tok()
                p.dop("tensor_scalar", [t_rq], qst[po][:, 12:16], qst[po][:, 8:12], 0.5, None,
                      ALU.mult)
                t_rq5 = p.dve.tok()
                gd[f"rq5_{po}"] = [t_rq5]
                p.pop("tensor_tensor", [t_eq, t_lam] + G(f"qh4b{pi}"),
                      qh4b[pi][:].rearrange("p (r f) -> p r f", r=4),
                      eq4[pi][:].rearrange("p (r f) -> p r f", r=4),
                      lam[pi][:].unsqueeze(1).broadcast_to([128, 4, 128]), ALU.mult)
                t_qhb = p.pool.tok()
                gd[f"eq4_{pi}"] = [t_qhb]
                gd[f"lam{pi}"] = [t_qhb]
                for hh in range(4):
                    bkb = hh % 2
                    p.transp([t_qhb, pbq_free[bkb]], pb[bkb][:, 0:128],
                             qh4b[pi][:, hh * 128:(hh + 1) * 128], c_idb[:])
                    t_tt = p.pe.tok()
                    p.dop("tensor_copy", [t_tt] + (G(f"qh4f{po}") if hh == 0 else []),
                          qh4f[po][:, hh * 128:(hh + 1) * 128], pb[bkb][:, 0:128])
                    pbq_free[bkb] = p.dve.tok()
                gd[f"qh4f{po}_w"] = [p.dve.tok()]
                gd[f"qh4b{pi}"] = [p.pe.tok()]

            def scores(u, hh, bank):
                i, g = divmod(u, NG)
                po = u % 3
                hp = hh % 2
                nkb = i // 4 + 1
                for kb in range(nkb):
                    j0 = kb * 4
                    j1 = min(i + 1, j0 + 4)
                    for j in range(j0, j1):
                        p.mm(G(f"sq4_{po}_w") + [skg_tok[u]] + G(f"pf{bank}") if j == j0 else [],
                             pf[bank][:, (j - j0) * 128:(j - j0 + 1) * 128],
                             sk_g[g][:, j * 128:(j + 1) * 128],
                             sq4[po][:, hh * 128:(hh + 1) * 128],
                             start=True, stop=(j < i))
                        if j == i:
                            p.mm([t_const], pf[bank][:, (j - j0) * 128:(j - j0 + 1) * 128],
                                 c_u2[:], c_u1[:], start=False, stop=True)
                    t_sc = p.pe.tok()
                    nk = (j1 - j0) * 128
                    p.aop([t_sc] + (G(f"pstr{hp}") if kb == 0 else []),
                          pstr[hp][:, j0 * 128: j0 * 128 + nk], pf[bank][:, 0:nk], AF.Exp,
                          scale=ISQD)
                    t_pex = p.act.tok()
                    gd[f"pf{bank}"] = [t_pex]
                gd[f"pexp{hp}"] = [t_pex]

            def headrest(u, hh):
                i, g = divmod(u, NG)
                po = u % 3
                hp = hh % 2
                t_pex = G(f"pexp{hp}")[0]
                for j in range(i + 1):
                    p.mm([t_pex, ld_tok[u]] + G("pf1") if j == 0 else [], pf[1][:, 0:129],
                         pstr[hp][:, j * 128:(j + 1) * 128],
                         v_g[g][:, j * VW: j * VW + 129],
                         start=(j == 0), stop=(j == i))
                t_ype = p.pe.tok()
                gd[f"pstr{hp}"] = [t_ype]
                p.mm(G(f"ktlf{po}_w") + G(f"qh4f{po}_w") + G("pf4"), pf[4][:, 0:128],
                     ktl_f[po][:], qh4f[po][:, hh * 128:(hh + 1) * 128], start=True, stop=True)
                t_pt = p.pe.tok()
                p.dop("tensor_mul", [t_pt, t_const] + G(f"ptm{hp}"), ptm[hp][:],
                      pf[4][:, 0:128], c_mkeep[:])
                t_ptm = p.dve.tok()
                p.mm([t_ptm], pf[4][:, 128:256], ptm[hp][:],
                     v_g[g][:, i * VW: i * VW + 128], start=True, stop=False)
                p.mm(G(f"s_b_w{g}"), pf[4][:, 128:256], qh4f[po][:, hh * 128:(hh + 1) * 128],
                     s_b[g][:], start=False, stop=True)
                t_gpe = p.pe.tok()
                gd[f"ptm{hp}"] = [t_gpe]
                p.dop("reciprocal", [t_ype] + G(f"rzy{hp}"), zyp[hp][:, 0:1], pf[1][:, 128:129])
                t_rz = p.dve.tok()
                p.dop("tensor_scalar", [t_rz] + G(f"osb{hp}"), o_sb[hp][:], pf[1][:, 0:128],
                      zyp[hp][:, 0:1], 0.5, ALU.mult, ALU.mult)
                t_o1 = p.dve.tok()
                gd["pf1"] = [t_o1]
                gd[f"rzy{hp}"] = [t_o1]
                p.dop("scalar_tensor_tensor", [t_o1, t_gpe] + G(f"rq5_{po}"), o_sb[hp][:],
                      pf[4][:, 128:256], qst[po][:, 12 + hh:13 + hh], o_sb[hp][:],
                      ALU.mult, ALU.add)
                t_o2 = p.dve.tok()
                gd["pf4"] = [t_o2]
                p.transp([t_o2, pbq_free[hp]], pb[hp][:, 0:128], o_sb[hp][:], c_idb[:])
                t_oT = p.pe.tok()
                gd[f"osb{hp}"] = [t_oT]
                p.dop("tensor_copy", [t_oT] + G(f"ostg_dma{u % 2}"),
                      ostg[u % 2][:, hh * 128:(hh + 1) * 128], pb[hp][:, 0:128])
                pbq_free[hp] = p.dve.tok()

            def state(u):
                i, g = divmod(u, NG)
                po = u % 3
                p.mm(G(f"ktlb{po}_w") + [ld_tok[u]] + G("pf3"), pf[3][:, 0:128],
                     ktl_b[po][:], v_g[g][:, i * VW:i * VW + 128], start=True, stop=True)
                t_kv = p.pe.tok()
                gd[f"ktlb{po}"] = [t_kv]
                p.dop("tensor_add", [t_kv] + G(f"s_f_w{g}") + G("stmp"), stmp[:],
                      pf[3][:, 0:128], s_f[g][:])
                t_st = p.dve.tok()
                gd["pf3"] = [t_st]
                p.dop("tensor_scalar", [t_st] + G(f"lamC{po}_w"), s_f[g][:], stmp[:],
                      lamC[po][:], None, ALU.mult)
                t_sf = p.dve.tok()
                gd[f"s_f_w{g}"] = [t_sf]
                gd["stmp"] = [t_sf]
                gd[f"lamC{po}"] = [t_sf]
                p.aop([t_sf, p.pe.tok()], s_b[g][:], s_f[g][:], AF.Copy)
                gd[f"s_b_w{g}"] = [p.act.tok()]
                gd[f"ktlf{po}"] = [p.pe.tok()]

            load_u(0)
            load_u(1)
            load_u(2)
            load_u(3)
            tail(0)
            tail(1)
            for u in range(NU):
                i, g = divmod(u, NG)
                scores(u, 0, 5)
                scores(u, 1, 2)
                if u + 4 < NU:
                    load_u(u + 4)
                headrest(u, 0)
                scores(u, 2, 5)
                headrest(u, 1)
                scores(u, 3, 2)
                headrest(u, 2)
                headrest(u, 3)
                gd[f"sq4_{u % 3}"] = [p.pe.tok()]
                gd[f"qh4f{u % 3}"] = [p.pe.tok()]
                t_oe = p.dve.tok()
                p.qs.wait(t_oe)
                t_od = p.qs.dma(o_scr[g, i, :, :], ostg[u % 2][:])
                gd[f"ostg_dma{u % 2}"] = [t_od]
                if u + 2 < NU:
                    tail(u + 2)
                state(u)

        p.barrier()

        if "d_o" in dbg_out:
            p.qs.dma(dbg_out["d_o"], o_scr.rearrange("g i p c -> (g i p) c"))
            p.barrier()
        f_pf0 = f_pf1 = f_pf2 = f_pf3 = f_pf4 = f_pf5 = None

        mark("p4")
        stA.close()
        stB = ExitStack()
        stB4 = ExitStack()
        x_mid = sb(stB, nc, "x_mid", [128, NTH * D], F32)
        comb_tok = sb(stB, nc, "comb_tok", [128, NT * E], F32)
        h2s = sb(stB, nc, "h2s", [128, NF * TH], BF16)
        h2h = [h2s, h2s]
        wo_full = sbr(stB4, nc, "wo_full", [128, NF * D], BF16)
        wrt_sb = sbr(stB4, nc, "wrt_sb", [128, NF * E], BF16)
        o_t = [sbr(stB4, nc, f"o_t{i}", [128, D], BF16) for i in range(2)]
        xb4 = [sbr(stB4, nc, f"xb4_{i}", [128, D], F32) for i in range(2)]
        h2tok = sbr(stB4, nc, "h2tok", [128, D], BF16)
        logits_sb = sbr(stB4, nc, "logits_sb", [128, NT * E], F32)
        xsum = sbr(stB4, nc, "xsum", [128, NT * 4], F32)
        lstat = sbr(stB4, nc, "lstat", [128, 8], F32)

        p.qs.wait(t_wcq)
        for fc in range(NF):
            p.qs.dma(wrt_sb[:, fc * E:(fc + 1) * E], wrtb[fc * 128:(fc + 1) * 128, :])
        for fc in range(NF):
            p.qs.dma(wo_full[:, fc * D:(fc + 1) * D], wob[fc * 128:(fc + 1) * 128, :])
        t_wo = ("q", p.qs.sem, p.qs.n)

        f_ot = [None, None]
        f_xb = [None, None]
        f_h2t = []
        f_pfy = [None, None]
        for half in range(2):
            hbase = half * NTH
            for tt in range(NTH):
                gt = hbase + tt
                bi = tt % 2
                if f_ot[bi] is not None:
                    p.qg.wait(f_ot[bi])
                for g in range(NG):
                    p.qg.dma(o_t[bi][:, g * 512:(g + 1) * 512], o_scr[g, gt, :, :])
                if f_xb[bi] is not None:
                    p.qg.wait(f_xb[bi])
                tx = p.qg.dma(xb4[bi][:], x_in[gt * 128:(gt + 1) * 128, :])
                t_ld = tx
                for oc in range(4):
                    bk = 4 + oc % 2
                    for h in range(NH):
                        p.mm([t_wo, t_ld, f_pfy[oc % 2]] if h == 0 else [], pf[bk][:, 0:512],
                             o_t[bi][:, h * 128:(h + 1) * 128],
                             wo_full[:, h * D + oc * 512: h * D + (oc + 1) * 512],
                             start=(h == 0), stop=(h == NH - 1))
                    t_mm = p.pe.tok()
                    p.dop("scalar_tensor_tensor", [t_mm, tx],
                          x_mid[:, tt * D + oc * 512: tt * D + (oc + 1) * 512],
                          pf[bk][:, 0:512], 1.0, xb4[bi][:, oc * 512:(oc + 1) * 512],
                          ALU.mult, ALU.add,
                          accum_out=xsum[:, gt * 4 + oc: gt * 4 + oc + 1])
                    f_pfy[oc % 2] = p.dve.tok()
                f_ot[bi] = p.pe.tok()
                t_xm = p.dve.tok()
                f_xb[bi] = t_xm
                if half == 0:
                    p.qg.wait(t_xm)
                    p.qg.dma(xmid_hbm[tt * 128:(tt + 1) * 128, :], x_mid[:, tt * D:(tt + 1) * D])
                p.dop("reduce_sum", [t_xm], lstat[:, 0:1], xsum[:, gt * 4:(gt + 1) * 4], axis=AX)
                t0 = p.dve.tok()
                p.dop("tensor_scalar", [t0], lstat[:, 1:2], lstat[:, 0:1], -1.0 / D, None, ALU.mult)
                t1 = p.dve.tok()
                p.aop([t1] + f_h2t, h2tok[:], x_mid[:, tt * D:(tt + 1) * D], AF.Square,
                      bias=lstat[:, 1:2], accum_out=lstat[:, 2:3])
                t2 = p.act.tok()
                p.dop("tensor_scalar", [t2], lstat[:, 3:4], lstat[:, 2:3], 1.0 / D, None, ALU.mult)
                t3 = p.dve.tok()
                p.aop([t3], lstat[:, 4:5], lstat[:, 3:4], AF.Ln, bias=EPS)
                t4 = p.act.tok()
                p.aop([t4], lstat[:, 5:6], lstat[:, 4:5], AF.Exp, scale=-0.5)
                t5 = p.act.tok()
                p.dop("tensor_scalar", [t5], h2tok[:], x_mid[:, tt * D:(tt + 1) * D],
                      lstat[:, 1:2], lstat[:, 5:6], ALU.add, ALU.mult)
                t_h2 = p.dve.tok()
                for fc in range(NF):
                    bkb = fc % 2
                    p.transp([t_h2, pbq_free[bkb]], pb[bkb][:, 0:128],
                             h2tok[:, fc * 128:(fc + 1) * 128], c_idb[:])
                    t_tt2 = p.pe.tok()
                    p.dop("tensor_copy", [t_tt2],
                          h2h[half][:, fc * TH + tt * 128: fc * TH + (tt + 1) * 128],
                          pb[bkb][:, 0:128])
                    pbq_free[bkb] = p.dve.tok()
                t_h2f = p.dve.tok()
                f_h2t = [p.pe.tok()]
                for fc in range(NF):
                    p.mm([t_h2f, t_const, f_pf3] if fc == 0 else [], pf[3][:, 0:E],
                         h2h[half][:, fc * TH + tt * 128: fc * TH + (tt + 1) * 128],
                         wrt_sb[:, fc * E:(fc + 1) * E], start=(fc == 0), stop=(fc == NF - 1))
                t_lg = p.pe.tok()
                p.dop("tensor_copy", [t_lg], logits_sb[:, gt * E:(gt + 1) * E], pf[3][:, 0:E])
                f_pf3 = p.dve.tok()
            if half == 0:
                th2 = p.dve.tok()
                p.qs.wait(th2)
                th2s = p.qs.dma(h2_hbm, h2s[:])
                p.dve.wait(th2s)

        if "d_xmid" in dbg_out:
            p.barrier()
            p.qs.dma(dbg_out["d_xmid"][0:TH, :], xmid_hbm[:, :])
            for tt in range(NTH):
                p.qs.dma(dbg_out["d_xmid"][TH + tt * 128:TH + (tt + 1) * 128, :],
                         x_mid[:, tt * D:(tt + 1) * D])
            p.qs.dma(dbg_out["d_logits"], logits_sb[:])
        p.barrier()

        if upto != "p4":
            mark("pc")
            with ExitStack() as stPC:
                gate = sbr(stPC, nc, "gate", [128, E], F32)
                m8 = sbr(stPC, nc, "m8", [128, E], F32)
                idxu = sbr(stPC, nc, "idxu", [128, E], mybir.dt.uint32)
                idxf = sbr(stPC, nc, "idxf", [128, E], F32)
                oh1 = sbr(stPC, nc, "oh1", [128, E], F32)
                gst = sbr(stPC, nc, "gst", [128, 4], F32)
                f_cb = []
                for gt in range(NT):
                    lsl = logits_sb[:, gt * E:(gt + 1) * E]
                    p.dop("reduce_max", [], gst[:, 0:1], lsl, axis=AX, negate=True)
                    t0 = p.dve.tok()
                    p.aop([t0] + f_cb, gate[:], lsl, AF.Exp, bias=gst[:, 0:1],
                          accum_out=gst[:, 1:2])
                    t1 = p.act.tok()
                    p.dop("reciprocal", [t1], gst[:, 2:3], gst[:, 1:2])
                    t2 = p.dve.tok()
                    p.dop("tensor_scalar", [t2], gate[:], gate[:], gst[:, 2:3], None, ALU.mult)
                    t3 = p.dve.tok()
                    p.dop("max", [t3], m8[:], gate[:])
                    t4 = p.dve.tok()
                    p.dop("max_index", [t4], idxu[:], m8[:], gate[:])
                    t5 = p.dve.tok()
                    p.dop("tensor_copy", [t5], idxf[:], idxu[:])
                    t6 = p.dve.tok()
                    p.dop("tensor_scalar", [t6, t_const], oh1[:], c_iota[:], idxf[:, 0:1],
                          None, ALU.is_equal)
                    t7 = p.dve.tok()
                    csl8 = comb_tok[:, gt * E:(gt + 1) * E]
                    p.dop("tensor_scalar", [t7], csl8, oh1[:], m8[:, 0:1], None, ALU.mult)
                    t8 = p.dve.tok()
                    p.dop("tensor_scalar", [t8], oh1[:], c_iota[:], idxf[:, 1:2], None, ALU.is_equal)
                    t9 = p.dve.tok()
                    p.dop("scalar_tensor_tensor", [t9], csl8, oh1[:], m8[:, 1:2], csl8,
                          ALU.mult, ALU.add)
                    f_cb = [p.dve.tok()]
                if "d_comb" in dbg_out:
                    p.qs.wait(p.dve.tok())
                    p.qs.dma(dbg_out["d_comb"], comb_tok[:])
            p.barrier()
            stB4.close()

        if upto in ("all", "p5e1", "p5h", "p5bc", "p5gu", "p5none", "p5nostore"):
            mark("p5")
            with ExitStack() as st5:
                gu = sbr(st5, nc, "gu", [128, NIC * TH], BF16)
                wgq = [sbr(st5, nc, f"wgq{i}", [128, NF * 384], BF16) for i in range(2)]
                wuq = [sbr(st5, nc, f"wuq{i}", [128, NF * 384], BF16) for i in range(2)]
                wdb = [sbr(st5, nc, f"wdb{i}", [128, NIC * 512], BF16) for i in range(2)]
                sgu = [sbr(st5, nc, f"sgu{i}", [128, 512], F32) for i in range(2)]
                halves = (1, 0) if upto == "all" else (1,)
                n_exp = 0 if upto in ("p5none",) else (E if upto in ("all", "p5h") else 1)
                QD = [(0, 3), (3, 3), (6, 3), (9, 2)]
                f_wg = [("e", p.pe.sem, 0), ("e", p.pe.sem, 0)]
                f_wd = [("e", p.pe.sem, 0), ("e", p.pe.sem, 0)]

                def load_q(e_, q_):
                    b = q_ % 2
                    ic0, nch = QD[q_]
                    iw = nch * 128
                    p.qs.wait(f_wg[b])
                    p.qs.wait(t_wcf)
                    for fc in range(NF):
                        p.qs.dma(wgq[b][:, fc * iw:(fc + 1) * iw],
                                 wgb[e_, fc * 128:(fc + 1) * 128, ic0 * 128:ic0 * 128 + iw])
                    for fc in range(NF):
                        p.qs.dma(wuq[b][:, fc * iw:(fc + 1) * iw],
                                 wub[e_, fc * 128:(fc + 1) * 128, ic0 * 128:ic0 * 128 + iw])
                    return ("q", p.qs.sem, p.qs.n)

                def load_d(e_, hs_):
                    b = hs_ % 2
                    p.qs.wait(f_wd[b])
                    for ic in range(NIC):
                        p.qs.dma(wdb[b][:, ic * 512:(ic + 1) * 512],
                                 wdbs[e_, ic * 128:(ic + 1) * 128, hs_ * 512:(hs_ + 1) * 512])
                    return ("q", p.qs.sem, p.qs.n)

                for half in halves:
                    hbase = half * NTH
                    if half == 0:
                        p.barrier()
                        for tt in range(NTH):
                            p.qg.dma(x_mid[:, tt * D:(tt + 1) * D], xmid_hbm[tt * 128:(tt + 1) * 128, :])
                        p.qg.dma(h2s[:], h2_hbm)
                        t_xr = ("q", p.qg.sem, p.qg.n)
                        p.dve.wait(t_xr)
                        p.pe.wait(t_xr)
                    t_gu_rd = ("e", p.pe.sem, 0)
                    f_sg = [[], []]
                    t_xev = p.dve.tok()
                    twg = load_q(0, 0) if n_exp else None
                    twd_pend = None
                    for e in range(n_exp):
                        for q in range(4):
                            b = q % 2
                            ic0, nch = QD[q]
                            iw = nch * 128
                            tw_cur = twg
                            if q < 3:
                                twg = load_q(e, q + 1)
                            elif e + 1 < n_exp:
                                twg = load_q(e + 1, 0)
                            for icl in range(nch):
                                ic = ic0 + icl
                                for s2 in range(TH // 512):
                                    gb = 0 if (icl + s2) % 2 == 0 else 2
                                    ub = gb + 1
                                    fg = f_pf0 if gb == 0 else f_pf2
                                    fu = f_pf1 if ub == 1 else f_pf3
                                    for fc in range(NF):
                                        p.mm([tw_cur, fg] if fc == 0 else [], pf[gb][:, 0:512],
                                             wgq[b][:, fc * iw + icl * 128: fc * iw + (icl + 1) * 128],
                                             h2h[half][:, fc * TH + s2 * 512: fc * TH + (s2 + 1) * 512],
                                             start=(fc == 0), stop=(fc == NF - 1))
                                    t_g = p.pe.tok()
                                    for fc in range(NF):
                                        p.mm([fu] if fc == 0 else [], pf[ub][:, 0:512],
                                             wuq[b][:, fc * iw + icl * 128: fc * iw + (icl + 1) * 128],
                                             h2h[half][:, fc * TH + s2 * 512: fc * TH + (s2 + 1) * 512],
                                             start=(fc == 0), stop=(fc == NF - 1))
                                    t_u2 = p.pe.tok()
                                    sp = (icl + s2) % 2
                                    p.aop([t_g] + f_sg[sp], sgu[sp][:], pf[gb][:, 0:512], AF.Silu)
                                    t_sg = p.act.tok()
                                    if gb == 0:
                                        f_pf0 = t_sg
                                    else:
                                        f_pf2 = t_sg
                                    p.dop("tensor_mul", [t_sg, t_u2, t_gu_rd],
                                          gu[:, ic * TH + s2 * 512: ic * TH + (s2 + 1) * 512],
                                          sgu[sp][:], pf[ub][:, 0:512])
                                    t_gf = p.dve.tok()
                                    if ub == 1:
                                        f_pf1 = t_gf
                                    else:
                                        f_pf3 = t_gf
                                    f_sg[sp] = [t_gf]
                            f_wg[b] = p.pe.tok()
                        t_gu = p.dve.tok()
                        twd = twd_pend if twd_pend is not None else load_d(e, 0)
                        for hs in range(4):
                            b = hs % 2
                            twd_cur = twd
                            if hs < 3:
                                twd = load_d(e, hs + 1)
                            for tt in range(NTH):
                                gt = hbase + tt
                                bk = 4 + tt % 2
                                fy = f_pf4 if bk == 4 else f_pf5
                                for ic in range(NIC):
                                    p.mm([twd_cur, t_gu, fy] if ic == 0 else [], pf[bk][:, 0:512],
                                         gu[:, ic * TH + tt * 128: ic * TH + (tt + 1) * 128],
                                         wdb[b][:, ic * 512:(ic + 1) * 512],
                                         start=(ic == 0), stop=(ic == NIC - 1))
                                t_y = p.pe.tok()
                                p.dop("scalar_tensor_tensor", [t_y, t_xev],
                                      x_mid[:, tt * D + hs * 512: tt * D + (hs + 1) * 512],
                                      pf[bk][:, 0:512], comb_tok[:, gt * E + e: gt * E + e + 1],
                                      x_mid[:, tt * D + hs * 512: tt * D + (hs + 1) * 512],
                                      ALU.mult, ALU.add)
                                if bk == 4:
                                    f_pf4 = p.dve.tok()
                                else:
                                    f_pf5 = p.dve.tok()
                            f_wd[b] = p.pe.tok()
                        twd_pend = None
                        t_xev = p.dve.tok()
                        t_gu_rd = p.pe.tok()
                    p.qg.wait(t_xev)
                    for tt in range(NTH):
                        gt = hbase + tt
                        p.qg.dma(out[gt * 128:(gt + 1) * 128, :], x_mid[:, tt * D:(tt + 1) * D])

            p.barrier()


        p.barrier()
        stB.close()
        mark("end")
    return nc



def _consts():
    idf = np.eye(128, dtype=np.float32)
    idb = idf.astype(ml_dtypes.bfloat16)
    s = np.arange(128)[:, None]
    t = np.arange(128)[None, :]
    tri16 = np.where(s <= t, -1.0 / 16.0, 0.0).astype(np.float32)
    mkeep = np.where(s <= t, 1.0, 0.0).astype(np.float32)
    mlow = np.where(t <= s, 1.0, 0.0).astype(ml_dtypes.bfloat16)
    u1 = np.where(s > t, 1.0, 0.0).astype(ml_dtypes.bfloat16)
    u2 = np.where(s <= t, -500.0, 0.0).astype(ml_dtypes.bfloat16)
    inv = (1.0 / ROPE_THETA ** (np.arange(0, HD, 2, dtype=np.float32) / HD)).astype(np.float64)
    ang = np.arange(T, dtype=np.float64)[:, None] * inv[None, :]
    cos = np.cos(ang).astype(np.float32).reshape(NT, 128, 64)
    sin = np.sin(ang).astype(np.float32).reshape(NT, 128, 64)
    cos_t = np.ascontiguousarray(cos.transpose(1, 0, 2).reshape(128, NT * 64))
    sin_t = np.ascontiguousarray(sin.transpose(1, 0, 2).reshape(128, NT * 64))
    iota8 = np.tile(np.arange(E, dtype=np.float32), (128, 1))
    cvals = np.tile(np.array([EPS, 0.5, -0.5, ISQD], dtype=np.float32), (128, 1))
    return dict(id_bf=idb, id_f=idf, tri16=tri16, mkeep=mkeep, mlow=mlow, u1c=u1, u2c=u2,
                cos_t=cos_t, sin_t=sin_t, iota8=iota8, cvals=cvals)


def make_in_maps(L0_attn_hidden_states, L0_attn_Wq, L0_attn_Wk, L0_attn_Wv, L0_attn_Wo,
                 L0_ffn_Wr, L0_ffn_Wg, L0_ffn_Wu, L0_ffn_Wd):
    consts = _consts()
    base = dict(consts)
    NF_, NG_ = 16, 4
    base["wq"] = np.ascontiguousarray(
        np.asarray(L0_attn_Wq, dtype=np.float32).reshape(NF_, 128, NG_, 512)
        .transpose(1, 2, 0, 3).reshape(128, NG_ * NF_ * 512))
    base["wk"] = np.ascontiguousarray(
        np.asarray(L0_attn_Wk, dtype=np.float32).reshape(NF_, 128, NG_, 128)
        .transpose(1, 2, 0, 3).reshape(128, NG_ * NF_ * 128))
    base["wv"] = np.ascontiguousarray(
        np.asarray(L0_attn_Wv, dtype=np.float32).reshape(NF_, 128, NG_, 128)
        .transpose(1, 2, 0, 3).reshape(128, NG_ * NF_ * 128))
    base["wo"] = np.ascontiguousarray(L0_attn_Wo, dtype=np.float32)
    base["wrt"] = np.ascontiguousarray(np.asarray(L0_ffn_Wr, dtype=np.float32).T)
    base["wg_s"] = np.ascontiguousarray(np.asarray(L0_ffn_Wg, dtype=np.float32).transpose(0, 2, 1))
    base["wu_s"] = np.ascontiguousarray(np.asarray(L0_ffn_Wu, dtype=np.float32).transpose(0, 2, 1))
    base["wd_s"] = np.ascontiguousarray(np.asarray(L0_ffn_Wd, dtype=np.float32).transpose(0, 2, 1))
    x = np.ascontiguousarray(L0_attn_hidden_states, dtype=np.float32)
    in_maps = []
    for c in range(8):
        m = dict(base)
        m["x"] = np.ascontiguousarray(x[c])
        in_maps.append(m)
    return in_maps


def kernel(**inputs):
    in_maps = make_in_maps(**inputs)
    nc = build_program()
    res = run_bass_kernel_spmd(nc, in_maps, core_ids=list(range(8)))
    return np.stack([np.asarray(r["out"], dtype=np.float32) for r in res.results], 0)
